# version 02
# baseline (speedup 1.0000x reference)
import numpy as np
import ml_dtypes
from contextlib import ExitStack

import concourse.bass as bass
import concourse.mybir as mybir
from concourse.alu_op_type import AluOpType as ALU

F32 = mybir.dt.float32
BF16 = mybir.dt.bfloat16
AF = mybir.ActivationFunctionType
AX = mybir.AxisListType.X

ENGINES = ("tensor", "vector", "scalar", "gpsimd", "sync")

NTOK = 2048
NCH = 4
NF = 8
NH, DH, MF = 16, 64, 256
RATIO = float(MF) ** -0.5
EPS_K = 1e-4
DNRM = float(DH) ** -0.25
WIN = 256
NW = NTOK // WIN
DI = 2730
NJC = 22
JCW = [128] * 21 + [42]
DEPTH = 6


def apbc(ap2d, dims):
    return bass.AP(ap2d.tensor, ap2d.offset, [list(ap2d.ap[0])] + [list(x) for x in dims])


class Op:
    __slots__ = ("eng", "fn", "waits", "inc", "val", "tag", "sem")

    def __init__(self, eng, fn, inc, tag):
        self.eng, self.fn, self.inc, self.tag = eng, fn, inc, tag
        self.waits = {}
        self.sem = eng


DMA_POOL = 8

class Prog:

    def __init__(self):
        self.ops = {e: [] for e in ENGINES}
        self.count = {e: 0 for e in ENGINES}
        self.dma_count = {}
        self.dma_idx = {e: 0 for e in ENGINES}
        self.last_w = {}
        self.readers = {}

    def _record(self, o, eng, reads, writes, accum, tok):
        self.ops[eng].append(o)
        for k in reads:
            self.readers.setdefault(k, []).append(tok)
        for k in writes:
            self.last_w[k] = tok
            if not accum:
                self.readers[k] = []

    def _deps(self, o, reads, writes, accum):
        deps = []
        for k in reads:
            tk = self.last_w.get(k)
            if tk is not None:
                deps.append(tk)
        for k in writes:
            tk = self.last_w.get(k)
            if tk is not None and not accum:
                deps.append(tk)
            for t2 in self.readers.get(k, ()):
                deps.append(t2)
        for (sname, v) in deps:
            o.waits[sname] = max(o.waits.get(sname, 0), v)

    def op(self, eng, fn, reads=(), writes=(), dma=False, accum=False, tag=""):
        if dma:
            i = self.dma_idx[eng] % DMA_POOL
            self.dma_idx[eng] += 1
            sname = f"dma_{eng}_{i}"
            prev = self.dma_count.get(sname, 0)
            o = Op(eng, fn, 16, tag)
            o.sem = sname
            self._deps(o, reads, writes, accum)
            if prev > 0:
                o.waits[sname] = max(o.waits.get(sname, 0), prev)
            self.dma_count[sname] = prev + 16
            o.val = prev + 16
            tok = (sname, o.val)
            self._record(o, eng, reads, writes, accum, tok)
            return tok
        o = Op(eng, fn, 1, tag)
        o.sem = eng
        self._deps(o, reads, writes, accum)
        self.count[eng] += 1
        o.val = self.count[eng]
        tok = (eng, o.val)
        self._record(o, eng, reads, writes, accum, tok)
        return tok

    def barrier_token(self):
        d = {e: self.count[e] for e in ENGINES if self.count[e] > 0}
        d.update(self.dma_count)
        return d

    def wait_all(self, eng, tokens, tag="bar"):
        o = Op(eng, lambda e: e.nop(), 1, tag)
        o.sem = eng
        o.waits = {sn: v for sn, v in tokens.items() if v > 0 and sn != eng}
        self.count[eng] += 1
        o.val = self.count[eng]
        self.ops[eng].append(o)
        return (eng, o.val)

    def n_ops(self):
        return sum(len(v) for v in self.ops.values())

    def emit(self, nc, sems):
        waited = {e: {} for e in ENGINES}

        def emit_engine(proxy, eng):
            w = waited[eng]
            for o in self.ops[eng]:
                for sname, v in sorted(o.waits.items()):
                    if v > w.get(sname, 0):
                        proxy.wait_ge(sems[sname], v)
                        w[sname] = v
                try:
                    o.fn(proxy).then_inc(sems[o.sem], o.inc)
                except Exception:
                    import sys
                    print(f"EMIT FAIL eng={eng} idx={o.val} tag={o.tag}", file=sys.stderr)
                    raise

        with nc.Block() as block:
            @block.tensor
            def _(t):
                emit_engine(t, "tensor")

            @block.vector
            def _(v):
                emit_engine(v, "vector")

            @block.scalar
            def _(s):
                emit_engine(s, "scalar")

            @block.gpsimd
            def _(g):
                emit_engine(g, "gpsimd")

            @block.sync
            def _(s):
                emit_engine(s, "sync")


class KB:
    def __init__(self, n_layers=DEPTH, dbg=()):
        self.nc = bass.Bass("TRN2", target_bir_lowering=False)
        self.P = Prog()
        self.es = ExitStack()
        self.n_layers = n_layers
        self.dbg = set(dbg)
        self.dram = {}
        self.t = {}

    def sb(self, name, shape, dt):
        h = self.es.enter_context(self.nc.sbuf_tensor(name, shape, dt))
        self.t[name] = h
        return h

    def ps(self, name, shape, dt=F32):
        h = self.es.enter_context(self.nc.psum_tensor(name, shape, dt))
        self.t[name] = h
        return h

    def din(self, name, shape, dt=F32):
        h = self.nc.dram_tensor(name, shape, dt, kind="ExternalInput")
        self.dram[name] = h
        return h

    def dout(self, name, shape, dt=F32):
        h = self.nc.dram_tensor(name, shape, dt, kind="ExternalOutput")
        self.dram[name] = h
        return h

    def dint(self, name, shape, dt=F32):
        h = self.nc.dram_tensor(name, shape, dt, kind="Internal")
        self.dram[name] = h
        return h

    def declare(self):
        d = self.din
        d("xfm", [1024, NTOK], F32)
        d("xall", [8, 1024, NTOK], F32)
        for w in ("wq", "wk", "wv", "wo"):
            d(w, [1024, 1024], F32)
        d("projT", [64, MF], F32)
        d("bo_row", [1, 1024], F32)
        d("cst", [128, 16], F32)
        d("cstrow", [4, 128], F32)
        d("ones512", [1, 512], F32)
        d("ident", [128, 128], F32)
        d("epat", [128, 8 * 16], F32)
        d("masks", [128, 3 * 384], F32)
        d("wqkv_l", [DEPTH, 1024, 3072], F32)
        d("wo_l", [DEPTH, 1024, 1024], F32)
        d("w1_l", [DEPTH, 1024, 5460], F32)
        d("w2_l", [DEPTH, DI, 1024], F32)
        d("qsrows", [DEPTH, 1, 128], F32)
        d("ksrows", [DEPTH, 1, 128], F32)
        d("lng_l", [DEPTH, 128, 8], F32)
        d("lnb_l", [DEPTH, 128, 8], F32)
        d("b1a_l", [DEPTH, 128, 22], F32)
        d("b1g_l", [DEPTH, 128, 22], F32)
        d("b2_l", [DEPTH, 128, 8], F32)
        d("rot_q", [NTOK, 128], F32)
        d("rot_k", [NTOK, 128], F32)
        self.dout("out_fm", [128, NF * NTOK], F32)
        self.dint("qmm_sp", [NH, 2, 128, NTOK], BF16)
        self.dint("xacc_sp", [128, NF * NTOK], F32)

        sb, ps = self.sb, self.ps
        sb("ySB", [128, NF * NTOK], F32)
        sb("BIGA", [128, 4096], F32)
        sb("BIGC", [128, 5248], F32)
        sb("BIGB", [128, 24576], BF16)
        sb("WA", [128, 8192], BF16)
        sb("WB", [128, 8192], BF16)
        sb("cstf", [128, 16], F32)
        sb("onescol_bf", [128, 1], BF16)
        sb("onesrow_f", [1, 128], F32)
        sb("onesrow_bf", [1, 128], BF16)
        sb("ones512_bf", [1, 512], BF16)
        sb("ident_bf", [128, 128], BF16)
        sb("ident_f", [128, 128], F32)
        sb("epat_bf", [128, 8 * 16], BF16)
        sb("masks_bf", [128, 3 * 384], BF16)
        sb("projTdn", [128, MF], BF16)
        sb("bo_bf", [1, 1024], BF16)
        sb("murow", [1, 512], F32)
        sb("rstdrow", [1, 512], F32)
        sb("negmubf", [1, 512], BF16)
        sb("r512a", [1, 512], F32)
        sb("r512b", [1, 512], F32)
        sb("rstd_tok", [128, 4], F32)
        sb("gsb_t", [1, 4096], BF16)
        sb("runmax", [128, 1], F32)
        sb("redcol", [128, 1], F32)
        sb("cwkcol", [128, 8], BF16)
        sb("stabcol", [128, 1], F32)
        sb("sq16", [128, 512], BF16)
        sb("diag_tok", [128, NH], F32)
        sb("negb", [128, NH], F32)
        sb("sumsq", [128, NH], F32)
        sb("invv", [128, NH], F32)
        sb("recipd", [128, NH], F32)
        sb("denom", [128, NH], F32)
        sb("lng_t", [128, 8], F32)
        sb("lnb_t", [128, 8], F32)
        sb("b1a_t", [128, 22], F32)
        sb("b1g_t", [128, 22], F32)
        sb("b2_t", [128, 8], F32)
        sb("edgec", [128, 28], F32)
        sb("mx2", [128, 2], F32)
        sb("dn2", [128, 2], F32)
        sb("rc2", [128, 2], F32)
        ps("ps_a", [128, 2048], F32)
        ps("ps_m0", [128, 512], F32)
        ps("ps_m1", [128, 512], F32)
        ps("ps_row", [128, 512], F32)
        ps("ps_tb", [128, 1024], BF16)

        self.sems = {}
        for e in ENGINES:
            self.sems[e] = self.es.enter_context(self.nc.semaphore("sem_" + e))
        for e in ("sync", "gpsimd", "scalar", "vector", "tensor"):
            for i in range(8):
                self.sems[f"dma_{e}_{i}"] = self.es.enter_context(
                    self.nc.semaphore(f"sdma_{e}_{i}"))

        B = self.t["BIGB"]
        self.xnbf = B[:, 0:16384]
        self.ptok = B[:, 16384:20480]
        self.chunk = B[:, 20480:24576]
        self.attcat = B[:, 0:16384]
        self.qmmt = B[:, 16384:20480]
        self.wqkv = B[:, 0:24576]
        self.xn2g = B[:, 0:4096]
        self.hgrp = B[:, 4096:15360]
        A = self.t["BIGA"]
        self.scr32 = A[:, 0:1024]
        self.dinvf = A[:, 1024:3072]
        self.xt = A[:, 3072:3584]
        self.xt2 = A[:, 3584:4096]
        C = self.t["BIGC"]
        CB = C[:, :].bitcast(BF16)
        self.vext = CB[:, 0:1040]
        self.ctxsb = C[:, 520:2600]
        self.expraw = C[:, 2600:3112]
        self.qmmstg = CB[:, 6224:7248]
        self.ctxbf = CB[:, 7248:9328]
        self.gsb = self.t["gsb_t"][0:1, :]
        self.qsbc = C[:, 4664:4792]
        self.ksbc = C[:, 4792:4920]
        self.zerorow = CB[0:1, 9840:9968]
        self.qs_f = C[0:1, 4984:5112]
        self.projT_f = C[:, 0:256]
        self.ks_f = C[0:1, 5112:5240]
        self.ysw = CB[:, 0:2048]
        self.qrtok = CB[:, 2048:3072]
        self.qF = CB[:, 3072:5120]
        self.attnw = CB[:, 5120:7168]
        self.rotw = A[:, 3072:3328]
        self.rots = A[:, 3328:3584]
        self.expn = CB[:, 8192:8576]
        self.expTn = CB[:, 8576:8960]
        self.expnB = CB[:, 7168:7552]
        self.expTnB = CB[:, 7552:7936]
        self.gelu16 = CB[:, 8960:9472]
        self.ttok1 = A[:, 0:1024]
        self.ttok2 = A[:, 1024:2048]
        self.ttok3 = A[:, 2048:3072]
        W = self.t["WB"]
        self.kF = W[:, 0:4096]
        self.vT = W[:, 4096:8192]

    def mm(self, out_ap, lhsT, rhs, reads, writes, start=True, stop=True,
           accum=False, tp=None, tag=""):
        self.P.op("tensor",
                  lambda e: e.matmul(out_ap, lhsT, rhs, start=start, stop=stop,
                                     tile_position=tp),
                  reads=reads, writes=writes, accum=accum, tag=tag)

    def bank_clear(self, psname, col, reads=(), tag="bclr"):
        t = self.t
        self.mm(t[psname][0:1, col:col + 1], t["onesrow_f"][0:1, 0:1],
                t["ident_f"][0:1, 0:1], list(reads) + ["ones_r", "ident_f"],
                [psname], start=True, stop=True, tag=tag)

    def claim(self, keys):
        self.P.op("vector", lambda e: e.memset(self.t["redcol"][:, :], 0.0),
                  reads=(), writes=list(keys) + ["redcol"], tag="claim")

    def zero_close(self, out_ap, okey, ncols):
        t = self.t
        self.mm(out_ap, self.zerorow, t["projTdn"][0:1, 0:ncols],
                ["zero_rb", "projTdn"], [okey], start=False, stop=True, accum=True)

    def cast_dma(self, out_ap, in_ap, reads, writes, tag=""):
        self.P.op("gpsimd", lambda e: e.dma_start(out=out_ap, in_=in_ap),
                  reads=reads, writes=writes, dma=True, tag=tag)

    def dma(self, out_ap, in_ap, reads, writes, tag="", eng="sync"):
        self.P.op(eng, lambda e: e.dma_start(out=out_ap, in_=in_ap),
                  reads=reads, writes=writes, dma=True, tag=tag)

    def vop(self, fn, reads, writes, tag=""):
        self.P.op("vector", fn, reads=reads, writes=writes, tag=tag)

    def sop(self, fn, reads, writes, tag=""):
        self.P.op("scalar", fn, reads=reads, writes=writes, tag=tag)

    def load_weight(self, dst_sb, dram_ap_2d, din_, reads, writes):
        k = din_ // 128
        self.cast_dma(dst_sb.rearrange("p (k m) -> p k m", k=k),
                      dram_ap_2d.rearrange("(k p) m -> p k m", p=128),
                      reads, writes)

    def transpose_col(self, src_row_ap, reads, dst_col_ap, wkey, pscol=0):
        t = self.t
        self.mm(t["ps_m1"][:, pscol:pscol + 1], src_row_ap, t["ident_f"][0:1, 0:1],
                list(reads) + ["ident_f"], ["ps_m1"])
        self.sop(lambda e: e.copy(dst_col_ap, t["ps_m1"][:, pscol:pscol + 1]),
                 ["ps_m1"], [wkey])

    def p0(self):
        t, d = self.t, self.dram
        self.dma(t["cstf"][:, :], d["cst"][:, :], [], ["cst"])
        self.dma(t["onesrow_f"][:, :], d["cstrow"][0:1, :], [], ["ones_r"])
        self.cast_dma(t["onesrow_bf"][:, :], d["cstrow"][0:1, :], [], ["ones_rb"])
        self.cast_dma(self.zerorow, d["cstrow"][1:2, :], [], ["zero_rb"])
        self.cast_dma(t["ones512_bf"][:, :], d["ones512"][:, :], [], ["ones512"])
        self.sop(lambda e: e.copy(t["onescol_bf"][:, :], t["cstf"][:, 0:1]),
                 ["cst"], ["ones_c"])
        self.cast_dma(t["ident_bf"][:, :], d["ident"][:, :], [], ["ident_b"])
        self.dma(t["ident_f"][:, :], d["ident"][:, :], [], ["ident_f"])
        self.cast_dma(t["epat_bf"][:, :], d["epat"][:, :], [], ["epat"])
        self.cast_dma(t["masks_bf"][:, :], d["masks"][:, :], [], ["masks"])
        self.dma(self.projT_f[0:64, :], d["projT"][:, :], [], ["projT_f"])
        self.dma(self.projT_f[64:128, :], d["projT"][:, :], ["projT_f"], ["projT_f"],
                 eng="sync")
        self.cast_dma(t["bo_bf"][:, :], d["bo_row"][:, :], [], ["bo_bf"])
        self.sop(lambda e: e.activation(t["projTdn"][:, :], self.projT_f[:, :],
                                        AF.Copy, scale=DNRM),
                 ["projT_f"], ["projTdn"])
        self.cast_dma(self.xnbf.rearrange("p (f n) -> p f n", f=NF),
                      d["xfm"].ap().rearrange("(f p) n -> p f n", p=128), [], ["xnbf"])

    def stats_chunk(self, src, src_key, c, do_negmu=False, tok_stride=NTOK):
        t = self.t
        csl = slice(0, 512)
        for f in range(NF):
            fs = slice(f * tok_stride + c * 512, f * tok_stride + (c + 1) * 512)
            self.mm(t["ps_row"][0:1, :], t["onescol_bf"][:, :], src[:, fs],
                    ["ones_c", src_key], ["ps_row"], start=(f == 0), stop=(f == NF - 1),
                    accum=(f > 0))
        self.vop(lambda e: e.tensor_scalar(t["murow"][0:1, csl], t["ps_row"][0:1, :],
                                           1.0 / 1024.0, None, ALU.mult),
                 ["ps_row"], ["murow"])
        for f in range(NF):
            fs = slice(f * tok_stride + c * 512, f * tok_stride + (c + 1) * 512)
            self.sop(lambda e, fs=fs: e.activation(t["sq16"][:, :], src[:, fs], AF.Square),
                     [src_key], ["sq16"])
            self.mm(t["ps_row"][0:1, :], t["onescol_bf"][:, :], t["sq16"][:, :],
                    ["ones_c", "sq16"], ["ps_row"], start=(f == 0), stop=(f == NF - 1),
                    accum=(f > 0))
        self.vop(lambda e: e.tensor_scalar(t["r512a"][:, :], t["ps_row"][0:1, :],
                                           1.0 / 1024.0, None, ALU.mult),
                 ["ps_row"], ["r512a"])
        self.vop(lambda e: e.tensor_tensor(t["r512b"][:, :], t["murow"][0:1, csl],
                                           t["murow"][0:1, csl], ALU.mult),
                 ["murow"], ["r512b"])
        self.vop(lambda e: e.tensor_tensor(t["r512a"][:, :], t["r512a"][:, :],
                                           t["r512b"][:, :], ALU.subtract),
                 ["r512a", "r512b"], ["r512a"])
        self.vop(lambda e: e.tensor_scalar(t["r512a"][:, :], t["r512a"][:, :],
                                           1e-5, None, ALU.add),
                 ["r512a"], ["r512a"])
        self.sop(lambda e: e.activation(t["r512a"][:, :], t["r512a"][:, :], AF.Ln),
                 ["r512a"], ["r512a"])
        self.sop(lambda e: e.activation(t["rstdrow"][0:1, csl], t["r512a"][:, :],
                                        AF.Exp, scale=-0.5),
                 ["r512a"], ["rstdrow"])
        if do_negmu:
            self.vop(lambda e: e.tensor_scalar(t["negmubf"][0:1, csl], t["murow"][0:1, csl],
                                               -1.0, None, ALU.mult),
                     ["murow"], ["negmubf"])

    def p1(self):
        t = self.t
        for c in range(NCH):
            self.stats_chunk(self.xnbf, "xnbf", c)
            self.mm(t["ps_m0"][:, :], t["onesrow_f"][:, :], t["murow"][0:1, :],
                    ["murow", "ones_r"], ["ps_m0"])
            self.mm(t["ps_m1"][:, :], t["onesrow_f"][:, :], t["rstdrow"][0:1, :],
                    ["rstdrow", "ones_r"], ["ps_m1"])
            for f in range(NF):
                fs = slice(f * NTOK + c * 512, f * NTOK + (c + 1) * 512)
                self.dma(self.xt[:, :],
                         self.dram["xfm"].ap()[f * 128:(f + 1) * 128,
                                               c * 512:(c + 1) * 512],
                         [], ["xt"], eng="sync")
                self.vop(lambda e, fs=fs: e.tensor_tensor(
                    self.scr32[:, 0:512], self.xt[:, :], t["ps_m0"][:, :], ALU.subtract),
                    ["xt", "ps_m0"], ["scrA"])
                self.vop(lambda e, fs=fs: e.tensor_tensor(
                    t["ySB"][:, fs], self.scr32[:, 0:512], t["ps_m1"][:, :], ALU.mult),
                    ["scrA", "ps_m1"], ["ySB"])
        for f in range(NF):
            for c in range(NCH):
                fs = slice(f * NTOK + c * 512, f * NTOK + (c + 1) * 512)
                self.sop(lambda e, fs=fs: e.copy(self.xnbf[:, fs], t["ySB"][:, fs]),
                         ["ySB"], ["xnbf"])

    def p2_sweep(self):
        t, d = self.t, self.dram
        self.load_weight(t["WB"][:, :], d["wk"].ap(), 1024, [], ["WB"])
        for half in range(2):
            for kc in range(8):
                self.mm(t["ps_row"][0:1, :], t["onescol_bf"][:, :],
                        t["WB"][:, kc * 1024 + half * 512: kc * 1024 + half * 512 + 512],
                        ["ones_c", "WB"], ["ps_row"], start=(kc == 0), stop=(kc == 7),
                        accum=(kc > 0))
            dst, key = (t["r512a"], "r512a") if half == 0 else (t["r512b"], "r512b")
            self.vop(lambda e, dst=dst: e.tensor_copy(dst[:, :], t["ps_row"][0:1, :]),
                     ["ps_row"], [key])
        for j in range(8):
            src, skey = (t["r512a"], "r512a") if j < 4 else (t["r512b"], "r512b")
            off = (j % 4) * 128
            self.transpose_col(src[0:1, off:off + 128], [skey],
                               t["cwkcol"][:, j:j + 1], "cwkcol")
        for h in range(NH):
            base, col = (h % 2) * 64, h // 2
            self.mm(t["ps_row"][0:1, 0:MF], t["cwkcol"][base:base + 64, col:col + 1],
                    t["projTdn"][base:base + 64, :], ["cwkcol", "projTdn"], ["ps_row"])
            self.sop(lambda e, h=h: e.copy(self.gsb[0:1, h * MF:(h + 1) * MF],
                                           t["ps_row"][0:1, 0:MF]),
                     ["ps_row"], ["gsb"])
        self.vop(lambda e: e.memset(t["runmax"][:, :], -3.0e38), [], ["runmax"])
        for ob in range(8):
            for c in range(NCH):
                self.cast_dma(self.chunk.rearrange("p (f n) -> p f n", f=NF),
                              d["xall"][ob, :, c * 512:(c + 1) * 512]
                              .rearrange("(f p) n -> p f n", p=128),
                              ["chunk"], ["chunk"])
                self.stats_chunk(self.chunk, "chunk", 0, do_negmu=True, tok_stride=512)
                for i in range(4):
                    self.transpose_col(
                        t["rstdrow"][0:1, i * 128:(i + 1) * 128],
                        ["rstdrow"], t["rstd_tok"][:, i:i + 1], "rstd_tok")
                for mc in range(8):
                    pm, pk = (t["ps_m0"], "ps_m0") if mc % 2 == 0 else (t["ps_m1"], "ps_m1")
                    for kc in range(8):
                        self.mm(pm[:, :],
                                t["WB"][:, kc * 1024 + mc * 128: kc * 1024 + mc * 128 + 128],
                                self.chunk[:, kc * 512:(kc + 1) * 512],
                                ["WB", "chunk"], [pk], start=(kc == 0), stop=(kc == 7),
                                accum=(kc > 0))
                    self.sop(lambda e, mc=mc, pm=pm: e.copy(
                        self.ptok[:, mc * 512:(mc + 1) * 512], pm[:, :]),
                        [pk], ["ptok"])
                for i in range(4):
                    ns = slice(i * 128, (i + 1) * 128)
                    for hg in range(2):
                        for hh in range(8):
                            h = hg * 8 + hh
                            base, ft = (h % 2) * 64, h // 2
                            out = t["ps_a"][:, hh * MF:(hh + 1) * MF]
                            self.mm(out,
                                    self.ptok[base:base + 64,
                                              ft * 512 + i * 128: ft * 512 + (i + 1) * 128],
                                    t["projTdn"][base:base + 64, :],
                                    ["ptok", "projTdn"], ["ps_a"], start=True, stop=False,
                                    accum=(hh > 0))
                            self.mm(out, t["negmubf"][0:1, ns],
                                    self.gsb[0:1, h * MF:(h + 1) * MF],
                                    ["negmubf", "gsb"], ["ps_a"], start=False, stop=True,
                                    accum=True)
                        self.vop(lambda e: e.tensor_reduce(
                            t["redcol"][:, :], t["ps_a"][:, :], AX, ALU.max),
                            ["ps_a"], ["redcol"])
                        self.vop(lambda e, i=i: e.tensor_tensor(
                            t["redcol"][:, :], t["redcol"][:, :], t["rstd_tok"][:, i:i + 1],
                            ALU.mult),
                            ["redcol", "rstd_tok"], ["redcol"])
                        self.vop(lambda e: e.tensor_tensor(
                            t["runmax"][:, :], t["runmax"][:, :], t["redcol"][:, :],
                            ALU.max),
                            ["redcol", "runmax"], ["runmax"])
        self.mm(t["ps_row"][0:1, 0:128], t["runmax"][:, 0:1], t["ident_f"][:, :],
                ["runmax", "ident_f"], ["ps_row"])
        self.vop(lambda e: e.tensor_reduce(t["r512a"][0:1, 0:1], t["ps_row"][0:1, 0:128],
                                           AX, ALU.max),
                 ["ps_row"], ["r512a"])
        self.mm(t["ps_m0"][:, 0:1], t["onesrow_f"][:, :], t["r512a"][0:1, 0:1],
                ["r512a", "ones_r"], ["ps_m0"])
        self.vop(lambda e: e.tensor_copy(t["stabcol"][:, :], t["ps_m0"][:, 0:1]),
                 ["ps_m0"], ["stabcol"])

    def qk_proj_chunk(self, wbuf, wkey, c):
        t = self.t
        for mc in range(8):
            pm, pk = (t["ps_m0"], "ps_m0") if mc % 2 == 0 else (t["ps_m1"], "ps_m1")
            for kc in range(8):
                self.mm(pm[:, :],
                        wbuf[:, kc * 1024 + mc * 128: kc * 1024 + mc * 128 + 128],
                        self.xnbf[:, kc * NTOK + c * 512: kc * NTOK + (c + 1) * 512],
                        [wkey, "xnbf"], [pk], start=(kc == 0), stop=(kc == 7),
                        accum=(kc > 0))
            self.sop(lambda e, mc=mc, pm=pm: e.copy(
                self.chunk[:, mc * 512:(mc + 1) * 512], pm[:, :]),
                [pk], ["chunk"])

    def diag_tok128(self, c, i):
        t = self.t
        if i == 0:
            for f in range(NF):
                self.sop(lambda e, f=f: e.activation(
                    t["sq16"][:, :], self.chunk[:, f * 512:(f + 1) * 512], AF.Square),
                    ["chunk"], ["sq16"])
                self.mm(t["ps_m0"][0:16, :], t["epat_bf"][:, f * 16:(f + 1) * 16],
                        t["sq16"][:, :], ["epat", "sq16"], ["ps_m0"],
                        start=(f == 0), stop=(f == NF - 1), accum=(f > 0))
            self.sop(lambda e: e.copy(self.scr32[0:16, 0:512], t["ps_m0"][0:16, :]),
                     ["ps_m0"], ["scrA"])
        self.mm(t["ps_m1"][:, 0:16], self.scr32[0:16, i * 128:(i + 1) * 128],
                t["ident_f"][0:16, 0:16], ["scrA", "ident_f"], ["ps_m1"])
        self.sop(lambda e: e.activation(t["diag_tok"][:, :], t["ps_m1"][:, 0:16],
                                        AF.Copy, scale=0.5 * DNRM * DNRM),
                 ["ps_m1"], ["diag_tok"])

    def p2_q(self):
        t, d = self.t, self.dram
        self.load_weight(t["WA"][:, :], d["wq"].ap(), 1024, ["WA"], ["WA"])
        for c in range(NCH):
            self.qk_proj_chunk(t["WA"], "WA", c)
            for i in range(4):
                self.diag_tok128(c, i)
                for hg in range(2):
                    for hh in range(8):
                        h = hg * 8 + hh
                        base, ft = (h % 2) * 64, h // 2
                        self.mm(t["ps_a"][:, hh * MF:(hh + 1) * MF],
                                self.chunk[base:base + 64,
                                           ft * 512 + i * 128: ft * 512 + (i + 1) * 128],
                                t["projTdn"][base:base + 64, :],
                                ["chunk", "projTdn"], ["ps_a"], accum=(hh > 0),
                                start=True, stop=False)
                        self.zero_close(t["ps_a"][:, hh * MF:(hh + 1) * MF], "ps_a", MF)
                    self.vop(lambda e, hg=hg: e.tensor_reduce(
                        t["negb"][:, hg * 8:hg * 8 + 8],
                        t["ps_a"][:, :].rearrange("p (h m) -> p h m", h=8),
                        AX, ALU.max, negate=True),
                        ["ps_a"], ["negb"])
                    self.vop(lambda e, hg=hg: e.tensor_tensor(
                        t["negb"][:, hg * 8:hg * 8 + 8],
                        t["negb"][:, hg * 8:hg * 8 + 8],
                        t["diag_tok"][:, hg * 8:hg * 8 + 8], ALU.subtract),
                        ["negb", "diag_tok"], ["negb"])
                    for hh in range(8):
                        h = hg * 8 + hh
                        self.sop(lambda e, hh=hh, h=h: e.activation(
                            self.expraw[:, 0:MF], t["ps_a"][:, hh * MF:(hh + 1) * MF],
                            AF.Exp, bias=t["negb"][:, h:h + 1]),
                            ["ps_a", "negb"], ["expraw"])
                        self.vop(lambda e, h=h: e.tensor_scalar(
                            self.ptok[:, h * MF:(h + 1) * MF], self.expraw[:, 0:MF],
                            RATIO, RATIO * EPS_K, ALU.mult, ALU.add),
                            ["expraw"], ["ptok"])
                tok = c * 4 + i
                for r in range(4):
                    mc, h0 = r // 2, (r % 2) * 8
                    for idx in range(8):
                        h = h0 + idx
                        self.P.op("tensor", lambda e, h=h, mc=mc, idx=idx: e.transpose(
                            t["ps_tb"][:, idx * 128:(idx + 1) * 128],
                            self.ptok[:, h * MF + mc * 128: h * MF + (mc + 1) * 128],
                            t["ident_bf"][:, :]),
                            reads=["ptok", "ident_b"], writes=["ps_tb"],
                            accum=(idx > 0), tag="qT")
                    self.sop(lambda e: e.copy(self.qmm_stage(), t["ps_tb"][:, :]),
                             ["ps_tb"], ["ptok_stage"])
                    self.dma(d["qmm_sp"].ap()[h0:h0 + 8, mc, :, tok * 128:(tok + 1) * 128]
                             .rearrange("h p n -> p h n"),
                             self.qmm_stage().rearrange("p (h n) -> p h n", h=8),
                             ["ptok_stage"], ["qmm_sp"], eng="sync")

    def qmm_stage(self):
        return self.qmmstg

    def ctx_slot(self, s):
        t = self.t
        if s < 28:
            return (t["ps_a"][:, (s // 7) * 512 + (s % 7) * 65:
                              (s // 7) * 512 + (s % 7) * 65 + 65], "ps_a")
        return (t["ps_row"][:, (s - 28) * 65:(s - 28) * 65 + 65], "ps_row")

    def p2_kv(self):
        t, d = self.t, self.dram
        self.load_weight(t["WA"][:, :], d["wv"].ap(), 1024, ["WA"], ["WA"])
        for h in range(NH):
            self.vop(lambda e, h=h: e.memset(self.vext[:, h * 65 + 64: h * 65 + 65], 1.0),
                     [], ["vext"])
        self.vop(lambda e: e.memset(self.ctxsb[:, :], 0.0), [], ["ctx_sb"])
        for c in range(NCH):
            self.qk_proj_chunk(t["WB"], "WB", c)
            for i in range(4):
                tok = c * 4 + i
                self.diag_tok128(c, i)
                self.vop(lambda e: e.tensor_scalar(
                    t["negb"][:, :], t["diag_tok"][:, :], t["stabcol"][:, 0:1], -1.0,
                    ALU.add, ALU.mult),
                    ["diag_tok", "stabcol"], ["negb"])
                for hg in range(2):
                    for hh in range(8):
                        h = hg * 8 + hh
                        base, ft = (h % 2) * 64, h // 2
                        kout = t["ps_m0"][:, 0:MF] if hh % 2 == 0 else t["ps_m1"][:, 0:MF]
                        kkey = "ps_m0" if hh % 2 == 0 else "ps_m1"
                        self.mm(kout,
                                self.chunk[base:base + 64,
                                           ft * 512 + i * 128: ft * 512 + (i + 1) * 128],
                                t["projTdn"][base:base + 64, :],
                                ["chunk", "projTdn"], [kkey], start=True, stop=False)
                        self.zero_close(kout, kkey, MF)
                        pm, pk = (t["ps_m0"], "ps_m0") if hh % 2 == 0 else (t["ps_m1"], "ps_m1")
                        self.sop(lambda e, h=h, pm=pm: e.activation(
                            self.expraw[:, 0:MF], pm[:, 0:MF],
                            AF.Exp, bias=t["negb"][:, h:h + 1]),
                            [pk, "negb"], ["expraw"])
                        self.vop(lambda e, h=h: e.tensor_scalar(
                            self.ptok[:, h * MF:(h + 1) * MF], self.expraw[:, 0:MF],
                            RATIO, RATIO * EPS_K, ALU.mult, ALU.add),
                            ["expraw"], ["ptok"])
                for half in range(2):
                    pm, pk = (t["ps_m0"], "ps_m0") if half == 0 else (t["ps_m1"], "ps_m1")
                    for kc in range(8):
                        self.mm(pm[:, :],
                                self.xnbf[:, kc * NTOK + tok * 128: kc * NTOK + (tok + 1) * 128],
                                t["WA"][:, kc * 1024 + half * 512: kc * 1024 + half * 512 + 512],
                                ["WA", "xnbf"], [pk], start=(kc == 0), stop=(kc == 7),
                                accum=(kc > 0))
                    for hh in range(8):
                        h = half * 8 + hh
                        self.sop(lambda e, h=h, hh=hh, pm=pm: e.copy(
                            self.vext[:, h * 65: h * 65 + 64], pm[:, hh * 64:(hh + 1) * 64]),
                            [pk], ["vext"])
                for h in range(NH):
                    for mcc in range(2):
                        s = h * 2 + mcc
                        out, okey = self.ctx_slot(s)
                        self.mm(out,
                                self.ptok[:, h * MF + mcc * 128: h * MF + (mcc + 1) * 128],
                                self.vext[:, h * 65:(h + 1) * 65],
                                ["ptok", "vext"], [okey], start=True, stop=True,
                                accum=(s > 0))
                psa_ap = apbc(t["ps_a"][:, 0:455], [[512, 4], [1, 455]])
                csb_ap = apbc(self.ctxsb[:, 0:455], [[455, 4], [1, 455]])
                self.vop(lambda e, psa_ap=psa_ap, csb_ap=csb_ap: e.tensor_tensor(
                    csb_ap, csb_ap, psa_ap, ALU.add),
                    ["ps_a", "ctx_sb"], ["ctx_sb"])
                self.vop(lambda e: e.tensor_tensor(
                    self.ctxsb[:, 1820:2080], self.ctxsb[:, 1820:2080],
                    t["ps_row"][:, 0:260], ALU.add),
                    ["ps_row", "ctx_sb"], ["ctx_sb"])
        self.sop(lambda e: e.copy(self.ctxbf[:, :], self.ctxsb[:, :]),
                 ["ctx_sb"], ["ctx_bf"])

    def p2_out(self):
        t, d = self.t, self.dram
        qa = self.qmmt[:, 0:2048]
        qb = self.qmmt[:, 2048:4096]
        for hp in range(8):
            for sub in range(2):
                h = 2 * hp + sub
                self.dma(qa, d["qmm_sp"].ap()[h, 0, :, :], ["qmm_sp", "ptok"],
                         ["ptok"], eng="sync")
                self.dma(qb, d["qmm_sp"].ap()[h, 1, :, :], ["qmm_sp", "ptok"],
                         ["ptok"], eng="sync")
                for nc_ in range(4):
                    if nc_ < 3:
                        slot, skey = t["ps_m0"][32 * nc_:32 * nc_ + 1, :], "ps_m0"
                    else:
                        slot, skey = t["ps_m1"][0:1, :], "ps_m1"
                    self.mm(slot, self.ctxbf[:, (2 * h) * 65 + 64:(2 * h) * 65 + 65],
                            qa[:, nc_ * 512:(nc_ + 1) * 512],
                            ["ctx_bf", "ptok"], [skey], start=True, stop=False)
                    self.mm(slot, self.ctxbf[:, (2 * h + 1) * 65 + 64:(2 * h + 1) * 65 + 65],
                            qb[:, nc_ * 512:(nc_ + 1) * 512],
                            ["ctx_bf", "ptok"], [skey], start=False, stop=True,
                            accum=True)
                sub1rows = [t["murow"], t["rstdrow"], t["r512a"], t["r512b"]]
                for nc_ in range(4):
                    if nc_ < 3:
                        slot, skey = t["ps_m0"][32 * nc_:32 * nc_ + 1, :], "ps_m0"
                    else:
                        slot, skey = t["ps_m1"][0:1, :], "ps_m1"
                    if sub == 0:
                        dst = self.dinvf[0:1, nc_ * 512:(nc_ + 1) * 512]
                    else:
                        dst = sub1rows[nc_][0:1, :]
                    self.vop(lambda e, slot=slot, dst=dst: e.reciprocal(dst, slot),
                             [skey], ["dinv"])
                for mcc in range(2):
                    qt = qa if mcc == 0 else qb
                    sidx = h * 2 + mcc
                    for nc_ in range(4):
                        self.mm(t["ps_a"][sub * 64:sub * 64 + 64,
                                          nc_ * 512:(nc_ + 1) * 512],
                                self.ctxbf[:, sidx * 65: sidx * 65 + 64],
                                qt[:, nc_ * 512:(nc_ + 1) * 512],
                                ["ctx_bf", "ptok"], ["ps_a"],
                                start=(mcc == 0), stop=(mcc == 1),
                                accum=(mcc > 0), tp=(0, sub * 64))
            sub1rows = [t["murow"], t["rstdrow"], t["r512a"], t["r512b"]]
            for nc_ in range(4):
                self.mm(t["ps_m0"][0:64, :], t["onesrow_f"][0:1, 0:64],
                        self.dinvf[0:1, nc_ * 512:(nc_ + 1) * 512],
                        ["dinv", "ones_r"], ["ps_m0"], tp=(0, 0))
                self.mm(t["ps_m0"][64:128, :], t["onesrow_f"][0:1, 0:64],
                        sub1rows[nc_][0:1, :],
                        ["dinv", "ones_r"], ["ps_m0"], accum=True, tp=(0, 64))
                self.vop(lambda e: e.tensor_copy(self.scr32[:, 0:512], t["ps_m0"][:, :]),
                         ["ps_m0"], ["scrA"])
                self.vop(lambda e, hp=hp, nc_=nc_: e.tensor_tensor(
                    self.attcat[:, hp * NTOK + nc_ * 512: hp * NTOK + (nc_ + 1) * 512],
                    t["ps_a"][:, nc_ * 512:(nc_ + 1) * 512],
                    self.scr32[:, 0:512], ALU.mult),
                    ["ps_a", "scrA"], ["xnbf"])
        self.load_weight(t["WB"][:, :], d["wo"].ap(), 1024, ["WB"], ["WB"])
        for mc in range(8):
            for nc_ in range(4):
                pm, pk = (t["ps_m0"], "ps_m0") if nc_ % 2 == 0 else (t["ps_m1"], "ps_m1")
                for kc in range(8):
                    self.mm(pm[:, :],
                            t["WB"][:, kc * 1024 + mc * 128: kc * 1024 + mc * 128 + 128],
                            self.attcat[:, kc * NTOK + nc_ * 512: kc * NTOK + (nc_ + 1) * 512],
                            ["WB", "xnbf"], [pk], start=(kc == 0), stop=False,
                            accum=(kc > 0))
                self.mm(pm[:, :], t["bo_bf"][0:1, mc * 128:(mc + 1) * 128],
                        t["ones512_bf"][0:1, :],
                        ["bo_bf", "ones512"], [pk], start=False, stop=True, accum=True)
                self.dma(self.xt[:, :],
                         d["xfm"].ap()[mc * 128:(mc + 1) * 128, nc_ * 512:(nc_ + 1) * 512],
                         ["xt"], ["xt"], eng="sync")
                self.vop(lambda e, pm=pm: e.tensor_tensor(
                    self.xt2[:, :], self.xt[:, :], pm[:, :], ALU.add),
                    ["xt", pk], ["xt2"])
                self.dma(d["xacc_sp"].ap()[:, mc * NTOK + nc_ * 512:
                                           mc * NTOK + (nc_ + 1) * 512],
                         self.xt2[:, :], ["xt2"], ["xacc_sp"], eng="sync")

    def p3_attn(self, li):
        t, d = self.t, self.dram
        self.claim(["wqkv", "xnbf", "ptok", "chunk", "xn2g", "hgrp", "WA",
                    "w2mc0", "w2mc1", "WB", "kF", "vT", "w1jc0", "w1jc1",
                    "ysw", "qrtok", "qF", "attnw", "ctx_bf", "vext", "ctx_sb",
                    "expraw", "ptok_stage", "expn", "expTn", "expnB",
                    "expTnB", "gelu16", "rotw", "rots", "xt", "xt2"])
        self.load_weight(self.wqkv, d["wqkv_l"].ap()[li], 1024, [], ["wqkv"])
        self.load_weight(t["WA"][:, :], d["wo_l"].ap()[li], 1024, [], ["WA"])
        self.dma(self.qs_f, d["qsrows"].ap()[li], [], ["qs_f"])
        self.dma(self.ks_f, d["ksrows"].ap()[li], [], ["ks_f"])
        for half in range(2):
            self.mm(t["ps_m0"][:, 0:64], t["onesrow_f"][:, :],
                    self.qs_f[0:1, half * 64:(half + 1) * 64],
                    ["qs_f", "ones_r"], ["ps_m0"])
            self.vop(lambda e, half=half: e.tensor_copy(
                self.qsbc[:, half * 64:(half + 1) * 64], t["ps_m0"][:, 0:64]),
                ["ps_m0"], ["qsbc"])
            self.mm(t["ps_m0"][:, 0:64], t["onesrow_f"][:, :],
                    self.ks_f[0:1, half * 64:(half + 1) * 64],
                    ["ks_f", "ones_r"], ["ps_m0"])
            self.vop(lambda e, half=half: e.tensor_copy(
                self.ksbc[:, half * 64:(half + 1) * 64], t["ps_m0"][:, 0:64]),
                ["ps_m0"], ["ksbc"])
        self.vop(lambda e: e.memset(self.kF[:, 2048:4096], 0.0),
                 [], ["kF"])
        self.vop(lambda e: e.memset(self.vT[:, 2048:4096], 0.0), [], ["vT"])
        for f in range(4, NF):
            self.sop(lambda e, f=f: e.copy(
                t["edgec"][:, (f - 4) * 7:(f - 4) * 7 + 7],
                apbc(t["ySB"][:, f * NTOK + WIN - 1: f * NTOK + WIN], [[WIN, 7]])),
                ["ySB"], ["edgec"])

        for w in range(NW):
            cur, prev = w % 2, 1 - (w % 2)
            wsl = slice(w * WIN, (w + 1) * WIN)
            for f in range(NF):
                dst = self.ysw[:, f * 256:(f + 1) * 256]
                if f < 4:
                    self.sop(lambda e, f=f, dst=dst, w=w: e.copy(
                        dst, t["ySB"][:, f * NTOK + w * WIN: f * NTOK + (w + 1) * WIN]),
                        ["ySB"], ["ysw"])
                elif w > 0:
                    self.sop(lambda e, f=f, dst=dst, w=w: e.copy(
                        dst[:, 0:1], t["edgec"][:, (f - 4) * 7 + w - 1:(f - 4) * 7 + w]),
                        ["edgec"], ["ysw"])
                    self.sop(lambda e, f=f, dst=dst, w=w: e.copy(
                        dst[:, 1:256], t["ySB"][:, f * NTOK + w * WIN: f * NTOK + w * WIN + 255]),
                        ["ySB"], ["ysw"])
                else:
                    self.vop(lambda e, dst=dst: e.memset(dst[:, 0:1], 0.0), [], ["ysw"])
                    self.sop(lambda e, f=f, dst=dst: e.copy(
                        dst[:, 1:256], t["ySB"][:, f * NTOK: f * NTOK + 255]),
                        ["ySB"], ["ysw"])
            for t2 in range(2):
                specs = [("q", t["ps_a"], 0, "ps_a"), ("k", t["ps_a"], 1024, "ps_a"),
                         ("v", None, 0, None)]
                for pi, (pname, ptens, poff, pkey) in enumerate(specs):
                    for sub in range(2):
                        if pname == "v":
                            ptens = t["ps_m0"] if sub == 0 else t["ps_m1"]
                            pkey = "ps_m0" if sub == 0 else "ps_m1"
                            out = ptens[:, :]
                        else:
                            out = ptens[:, poff + sub * 512: poff + (sub + 1) * 512]
                        for kc in range(8):
                            self.mm(out,
                                    self.ysw[:, kc * 256 + t2 * 128: kc * 256 + (t2 + 1) * 128],
                                    self.wqkv[:, kc * 3072 + pi * 1024 + sub * 512:
                                              kc * 3072 + pi * 1024 + (sub + 1) * 512],
                                    ["ysw", "wqkv"], [pkey], start=(kc == 0),
                                    stop=(kc == 7), accum=(kc > 0))
                vti = (cur * 2 + t2) * 1024
                self.sop(lambda e, vti=vti: e.copy(self.vT[:, vti:vti + 512],
                                                   t["ps_m0"][:, :]),
                         ["ps_m0"], ["vT"])
                self.sop(lambda e, vti=vti: e.copy(self.vT[:, vti + 512:vti + 1024],
                                                   t["ps_m1"][:, :]),
                         ["ps_m1"], ["vT"])
                for which in range(2):
                    psl = t["ps_a"][:, which * 1024:(which + 1) * 1024]
                    sbc = self.qsbc if which == 0 else self.ksbc
                    sbck = "qsbc" if which == 0 else "ksbc"
                    rot_d = d["rot_q"] if which == 0 else d["rot_k"]
                    self.sop(lambda e, psl=psl: e.activation(self.ttok3[:, :], psl,
                                                             AF.Square),
                             ["ps_a"], ["ttok3"])
                    self.vop(lambda e: e.tensor_reduce(
                        t["sumsq"][:, :],
                        self.ttok3[:, :].rearrange("p (h e) -> p h e", h=NH),
                        AX, ALU.add),
                        ["ttok3"], ["sumsq"])
                    self.vop(lambda e: e.tensor_scalar(
                        t["sumsq"][:, :], t["sumsq"][:, :], 1e-24, None, ALU.max),
                        ["sumsq"], ["sumsq"])
                    self.sop(lambda e: e.activation(t["sumsq"][:, :], t["sumsq"][:, :],
                                                    AF.Ln),
                             ["sumsq"], ["sumsq"])
                    self.sop(lambda e: e.activation(t["invv"][:, :], t["sumsq"][:, :],
                                                    AF.Exp, scale=-0.5),
                             ["sumsq"], ["invv"])
                    self.vop(lambda e, psl=psl: e.tensor_tensor(
                        self.ttok1[:, :], psl,
                        apbc(t["invv"][:, 0:1], [[1, NH], [0, DH]]), ALU.mult),
                        ["ps_a", "invv"], ["ttok1"])
                    self.dma(self.rotw[:, 0:128],
                             rot_d.ap()[w * WIN + t2 * 128: w * WIN + (t2 + 1) * 128, :],
                             [], ["rotw"])
                    self.vop(lambda e, sbc=sbc: e.tensor_tensor(
                        self.rots[:, 0:128], self.rotw[:, 0:128], sbc, ALU.mult),
                             ["rotw", sbck], ["rots"])
                    self.vop(lambda e: e.tensor_tensor(
                        self.ttok2[:, :], self.ttok1[:, :],
                        apbc(self.rots[:, 0:64], [[0, NH], [1, DH]]), ALU.mult),
                        ["ttok1", "rots"], ["ttok2"])
                    self.vop(lambda e: e.tensor_tensor(
                        apbc(self.ttok3[:, 0:32], [[64, NH], [1, 32]]),
                        apbc(self.ttok1[:, 32:64], [[64, NH], [1, 32]]),
                        apbc(self.rots[:, 64:96], [[0, NH], [1, 32]]), ALU.mult),
                        ["ttok1", "rots"], ["ttok3"])
                    self.vop(lambda e: e.tensor_tensor(
                        apbc(self.ttok3[:, 32:64], [[64, NH], [1, 32]]),
                        apbc(self.ttok1[:, 0:32], [[64, NH], [1, 32]]),
                        apbc(self.rots[:, 96:128], [[0, NH], [1, 32]]), ALU.mult),
                        ["ttok1", "rots"], ["ttok3"])
                    qrbuf, qrkey = (self.qrtok, "qrtok")
                    self.vop(lambda e, qrbuf=qrbuf: e.tensor_tensor(
                        qrbuf[:, :], self.ttok2[:, :], self.ttok3[:, :], ALU.add),
                        ["ttok2", "ttok3"], [qrkey])
                    for ftile in range(8):
                        self.P.op("tensor", lambda e, ftile=ftile, qrbuf=qrbuf: e.transpose(
                            t["ps_tb"][:, ftile * 128:(ftile + 1) * 128],
                            qrbuf[:, ftile * 128:(ftile + 1) * 128],
                            t["ident_bf"][:, :]),
                            reads=[qrkey, "ident_b"], writes=["ps_tb"],
                            accum=(ftile > 0), tag="p3T")
                    if which == 0:
                        dstap = apbc(self.qF[:, t2 * 128:t2 * 128 + 1], [[256, 8], [1, 128]])
                        dkey = "qF"
                    else:
                        dstap = apbc(self.kF[:, cur * 2048 + t2 * 128:cur * 2048 + t2 * 128 + 1], [[256, 8], [1, 128]])
                        dkey = "kF"
                    self.sop(lambda e, dstap=dstap: e.copy(dstap, t["ps_tb"][:, :]),
                             ["ps_tb"], [dkey])
            for c in range(2):
                for h in range(NH):
                    base, ftile = (h % 2) * 64, h // 2
                    nA = 256 - c * 128
                    p = (c * NH + h) % 2
                    psS, psSk = (t["ps_m0"], "ps_m0") if p == 0 else (t["ps_row"], "ps_row")
                    expnP, expnPk = (self.expn, "expn") if p == 0 else (self.expnB, "expnB")
                    expTnP, expTnPk = (self.expTn, "expTn") if p == 0 else (self.expTnB, "expTnB")
                    mxk, dnk, rck = f"mx{p}", f"dn{p}", f"rc{p}"
                    self.mm(psS[:, 0:nA],
                            self.qF[base:base + 64, ftile * 256 + c * 128:
                                    ftile * 256 + (c + 1) * 128],
                            self.kF[base:base + 64,
                                    prev * 2048 + ftile * 256 + c * 128:
                                    prev * 2048 + (ftile + 1) * 256],
                            ["qF", "kF"], [psSk], start=True, stop=False)
                    self.zero_close(psS[:, 0:nA], psSk, min(nA, MF))
                    self.mm(psS[:, nA:384],
                            self.qF[base:base + 64, ftile * 256 + c * 128:
                                    ftile * 256 + (c + 1) * 128],
                            self.kF[base:base + 64,
                                    cur * 2048 + ftile * 256:
                                    cur * 2048 + ftile * 256 + 128 + c * 128],
                            ["qF", "kF"], [psSk], start=True, stop=False, accum=True)
                    self.zero_close(psS[:, nA:384], psSk, min(384 - nA, MF))
                    self.vop(lambda e, psS=psS, p=p: e.tensor_reduce(
                        t["mx2"][:, p:p + 1], psS[:, 0:384], AX, ALU.max,
                        negate=True),
                        [psSk], [mxk])
                    self.sop(lambda e, psS=psS, expnP=expnP, p=p: e.activation(
                        expnP[:, :], psS[:, 0:384], AF.Exp,
                        bias=t["mx2"][:, p:p + 1]),
                        [psSk, mxk], [expnPk])
                    pat = 0 if w > 0 else (1 if c == 0 else 2)
                    self.vop(lambda e, pat=pat, expnP=expnP, p=p: e.scalar_tensor_tensor(
                        expnP[:, :], expnP[:, :], 1.0,
                        t["masks_bf"][:, pat * 384:(pat + 1) * 384],
                        ALU.bypass, ALU.mult,
                        accum_out=t["dn2"][:, p:p + 1]),
                        [expnPk, "masks"], [expnPk, dnk])
                    self.vop(lambda e, p=p: e.reciprocal(t["rc2"][:, p:p + 1],
                                                         t["dn2"][:, p:p + 1]),
                             [dnk], [rck])
                    self.vop(lambda e, expnP=expnP, p=p: e.tensor_scalar(
                        expnP[:, :], expnP[:, :], t["rc2"][:, p:p + 1], None,
                        ALU.mult),
                        [expnPk, rck], [expnPk])
                    for jc in range(3):
                        self.P.op("tensor", lambda e, jc=jc, expnP=expnP: e.transpose(
                            t["ps_tb"][:, jc * 128:(jc + 1) * 128],
                            expnP[:, jc * 128:(jc + 1) * 128],
                            t["ident_bf"][:, :]),
                            reads=[expnPk, "ident_b"], writes=["ps_tb"],
                            accum=(jc > 0), tag="eT")
                    self.sop(lambda e, expTnP=expTnP: e.copy(expTnP[:, :], t["ps_tb"][:, 0:384]),
                             ["ps_tb"], [expTnPk])
                    if c == 0:
                        tidx = [prev * 2 + 0, prev * 2 + 1, cur * 2 + 0]
                    else:
                        tidx = [prev * 2 + 1, cur * 2 + 0, cur * 2 + 1]
                    sub = h % 2
                    for jc in range(3):
                        self.mm(t["ps_m1"][sub * 64:sub * 64 + 64, 0:128],
                                self.vT[:, tidx[jc] * 1024 + h * DH:
                                        tidx[jc] * 1024 + (h + 1) * DH],
                                expTnP[:, jc * 128:(jc + 1) * 128],
                                ["vT", expTnPk], ["ps_m1"], start=(jc == 0),
                                stop=(jc == 2), accum=(jc > 0), tp=(0, sub * 64))
                    if sub == 1:
                        self.sop(lambda e, h=h, c=c: e.copy(
                            self.attnw[:, (h // 2) * 256 + c * 128:
                                       (h // 2) * 256 + (c + 1) * 128],
                            t["ps_m1"][:, 0:128]),
                            ["ps_m1"], ["attnw"])
            for mc in range(8):
                pm, pk = (t["ps_m0"], "ps_m0") if mc % 2 == 0 else (t["ps_m1"], "ps_m1")
                for kc in range(8):
                    self.mm(pm[:, 0:256],
                            t["WA"][:, kc * 1024 + mc * 128: kc * 1024 + mc * 128 + 128],
                            self.attnw[:, kc * 256:(kc + 1) * 256],
                            ["WA", "attnw"], [pk], start=(kc == 0), stop=(kc == 7),
                            accum=(kc > 0))
                self.vop(lambda e, mc=mc, pm=pm, w=w: e.tensor_tensor(
                    t["ySB"][:, mc * NTOK + w * WIN: mc * NTOK + (w + 1) * WIN],
                    t["ySB"][:, mc * NTOK + w * WIN: mc * NTOK + (w + 1) * WIN],
                    pm[:, 0:256], ALU.add),
                    ["ySB", pk], ["ySB"])

    def p3_ffn(self, li):
        t, d = self.t, self.dram
        self.claim(["ysw", "qrtok", "qF", "xn2g", "hgrp", "wqkv", "w1jc0",
                    "w1jc1", "kF", "vT", "w2mc0", "w2mc1", "WA", "WB"])
        ys2 = self.t["BIGC"][:, :].bitcast(BF16)[:, 0:4096]
        self.dma(t["lng_t"][:, :], d["lng_l"].ap()[li], [], ["lng_t"])
        self.dma(t["lnb_t"][:, :], d["lnb_l"].ap()[li], [], ["lnb_t"])
        self.dma(t["b1a_t"][:, :], d["b1a_l"].ap()[li], [], ["b1a_t"])
        self.dma(t["b1g_t"][:, :], d["b1g_l"].ap()[li], [], ["b1g_t"])
        self.dma(t["b2_t"][:, :], d["b2_l"].ap()[li], [], ["b2_t"])
        for f in range(4, NF):
            self.sop(lambda e, f=f: e.copy(
                t["edgec"][:, (f - 4) * 3:(f - 4) * 3 + 3],
                apbc(t["ySB"][:, f * NTOK + 511: f * NTOK + 512], [[512, 3]])),
                ["ySB"], ["edgec"])
        for g in range(4):
            g0 = g * 512
            for f in range(NF):
                dst = ys2[:, f * 512:(f + 1) * 512]
                wk = ["ysw"]
                if f < 4:
                    self.sop(lambda e, f=f, dst=dst, g0=g0: e.copy(
                        dst, t["ySB"][:, f * NTOK + g0: f * NTOK + g0 + 512]),
                        ["ySB"], wk)
                elif g > 0:
                    self.sop(lambda e, f=f, dst=dst, g=g: e.copy(
                        dst[:, 0:1], t["edgec"][:, (f - 4) * 3 + g - 1:(f - 4) * 3 + g]),
                        ["edgec"], wk)
                    self.sop(lambda e, f=f, dst=dst, g0=g0: e.copy(
                        dst[:, 1:512], t["ySB"][:, f * NTOK + g0: f * NTOK + g0 + 511]),
                        ["ySB"], wk)
                else:
                    self.vop(lambda e, dst=dst: e.memset(dst[:, 0:1], 0.0), [], wk)
                    self.sop(lambda e, f=f, dst=dst: e.copy(
                        dst[:, 1:512], t["ySB"][:, f * NTOK: f * NTOK + 511]),
                        ["ySB"], ["ysw"])
            self.stats_chunk(ys2, "ysw", 0, tok_stride=512)
            self.mm(t["ps_m0"][:, :], t["onesrow_f"][:, :], t["murow"][0:1, :],
                    ["murow", "ones_r"], ["ps_m0"])
            self.mm(t["ps_m1"][:, :], t["onesrow_f"][:, :], t["rstdrow"][0:1, :],
                    ["rstdrow", "ones_r"], ["ps_m1"])
            for f in range(NF):
                self.vop(lambda e, f=f: e.tensor_tensor(
                    self.scr32[:, 0:512], ys2[:, f * 512:(f + 1) * 512],
                    t["ps_m0"][:, :], ALU.subtract),
                    ["ysw", "ps_m0"], ["scrA"])
                self.vop(lambda e: e.tensor_tensor(
                    self.scr32[:, 512:1024], self.scr32[:, 0:512], t["ps_m1"][:, :],
                    ALU.mult),
                    ["scrA", "ps_m1"], ["scrA"])
                self.vop(lambda e, f=f: e.tensor_scalar(
                    self.xn2g[:, f * 512:(f + 1) * 512], self.scr32[:, 512:1024],
                    t["lng_t"][:, f:f + 1], t["lnb_t"][:, f:f + 1],
                    ALU.mult, ALU.add),
                    ["scrA", "lng_t", "lnb_t"], ["xn2g"])
            for jc in range(NJC):
                wdt = JCW[jc]
                buf = t["WB"][:, (jc % 2) * 4096:(jc % 2) * 4096 + 8 * wdt]
                bkey = f"w1jc{jc % 2}"
                self.cast_dma(buf.rearrange("p (k m) -> p k m", k=8),
                              d["w1_l"].ap()[li][:, jc * 128: jc * 128 + wdt]
                              .rearrange("(k p) m -> p k m", p=128),
                              [], [bkey], tag="w1")
                buf2 = t["WB"][:, (jc % 2) * 4096 + 2048:(jc % 2) * 4096 + 2048 + 8 * wdt]
                self.cast_dma(buf2.rearrange("p (k m) -> p k m", k=8),
                              d["w1_l"].ap()[li][:, DI + jc * 128: DI + jc * 128 + wdt]
                              .rearrange("(k p) m -> p k m", p=128),
                              [], [bkey], tag="w1g")
                for kc in range(8):
                    self.mm(t["ps_m0"][0:wdt, :], buf[:, kc * wdt:(kc + 1) * wdt],
                            self.xn2g[:, kc * 512:(kc + 1) * 512],
                            [bkey, "xn2g"], ["ps_m0"], start=(kc == 0), stop=(kc == 7),
                            accum=(kc > 0))
                    self.mm(t["ps_m1"][0:wdt, :], buf2[:, kc * wdt:(kc + 1) * wdt],
                            self.xn2g[:, kc * 512:(kc + 1) * 512],
                            [bkey, "xn2g"], ["ps_m1"], start=(kc == 0), stop=(kc == 7),
                            accum=(kc > 0))
                self.sop(lambda e, jc=jc, wdt=wdt: e.activation(
                    self.gelu16[0:wdt, :], t["ps_m1"][0:wdt, :], AF.Gelu,
                    bias=t["b1g_t"][0:wdt, jc:jc + 1]),
                    ["ps_m1", "b1g_t"], ["gelu16"])
                self.vop(lambda e, jc=jc, wdt=wdt: e.scalar_tensor_tensor(
                    self.hgrp[0:wdt, jc * 512:(jc + 1) * 512], t["ps_m0"][0:wdt, :],
                    t["b1a_t"][0:wdt, jc:jc + 1], self.gelu16[0:wdt, :],
                    ALU.add, ALU.mult),
                    ["ps_m0", "b1a_t", "gelu16"], ["hgrp"])
            for mc in range(8):
                wbuf = t["WA"][:, (mc % 2) * 4096:(mc % 2) * 4096 + 2816]
                wkey = f"w2mc{mc % 2}"
                self.cast_dma(wbuf[:, 0:21 * 128].rearrange("p (jc m) -> p jc m", jc=21),
                              d["w2_l"].ap()[li][0:2688, mc * 128:(mc + 1) * 128]
                              .rearrange("(jc p) m -> p jc m", p=128),
                              [], [wkey], tag="w2")
                self.cast_dma(wbuf[0:42, 21 * 128:22 * 128],
                              d["w2_l"].ap()[li][2688:2730, mc * 128:(mc + 1) * 128],
                              [wkey], [wkey], tag="w2r")
                ys = t["ps_a"][:, (mc % 4) * 512:(mc % 4 + 1) * 512]
                for jc in range(NJC):
                    wdt = JCW[jc]
                    self.mm(ys, wbuf[0:wdt, jc * 128:jc * 128 + 128],
                            self.hgrp[0:wdt, jc * 512:(jc + 1) * 512],
                            [wkey, "hgrp"], ["ps_a"], start=(jc == 0),
                            stop=(jc == NJC - 1), accum=(jc > 0))
                self.vop(lambda e, mc=mc, g0=g0, ys=ys: e.scalar_tensor_tensor(
                    t["ySB"][:, mc * NTOK + g0: mc * NTOK + g0 + 512], ys,
                    t["b2_t"][:, mc:mc + 1],
                    t["ySB"][:, mc * NTOK + g0: mc * NTOK + g0 + 512],
                    ALU.add, ALU.add),
                    ["ps_a", "b2_t", "ySB"], ["ySB"])

    def p4(self):
        t, d = self.t, self.dram
        self.claim(["rotw", "rots", "xt", "xt2"])
        for f in range(NF):
            for c in range(NCH):
                fs = slice(f * NTOK + c * 512, f * NTOK + (c + 1) * 512)
                self.dma(self.xt[:, :], d["xacc_sp"].ap()[:, fs], ["xacc_sp"], ["xt"],
                         eng="sync")
                self.vop(lambda e, fs=fs: e.tensor_tensor(
                    self.xt2[:, :], self.xt[:, :], t["ySB"][:, fs], ALU.add),
                    ["xt", "ySB"], ["xt2"])
                self.dma(d["out_fm"].ap()[:, fs], self.xt2[:, :], ["xt2"], ["out"],
                         eng="sync")
        self.P.wait_all("sync", self.P.barrier_token())

    def finish(self):
        self.P.emit(self.nc, self.sems)
        self.es.close()
        return self.nc


def rotary_tables_np():
    F32n = np.float32
    half = np.arange(0, DH, 2, dtype=F32n)
    inv_freq = (1.0 / 10000.0 ** (half / DH)).astype(F32n)
    tt = np.arange(NTOK, dtype=F32n)
    freqs = tt[:, None] * inv_freq[None, :]
    freqs = np.concatenate([freqs, freqs], -1)
    base = ((half + 0.4 * DH) / (1.4 * DH)).astype(F32n)
    power = ((tt - NTOK // 2) / (WIN // 2)).astype(F32n)
    scale = base[None, :] ** power[:, None]
    scale = np.concatenate([scale, scale], -1)
    cos, sin = np.cos(freqs).astype(F32n), np.sin(freqs).astype(F32n)
    sgn = np.concatenate([-np.ones(DH // 2, F32n), np.ones(DH // 2, F32n)])
    rot_q = np.concatenate([cos * scale * 8.0, sin * scale * sgn * 8.0], 1)
    rot_k = np.concatenate([cos / scale, sin / scale * sgn], 1)
    return np.ascontiguousarray(rot_q), np.ascontiguousarray(rot_k)


def band_masks_np():
    F32n = np.float32
    ii = np.arange(128)[:, None]
    jj = np.arange(384)[None, :]
    band = ((jj >= ii) & (jj <= ii + 256)).astype(F32n)
    w0c0 = band * (jj >= 256)
    w0c1 = band * (jj >= 128)
    return np.ascontiguousarray(np.concatenate([band, w0c0, w0c1], 1))


def prep_shared_inputs(I):
    F32n = np.float32
    out = {}
    x = np.asarray(I["L0_attn_x"], F32n)
    out["xall"] = np.ascontiguousarray(x.transpose(0, 2, 1))
    for w in ("wq", "wk", "wv", "wo"):
        out[w] = np.ascontiguousarray(np.asarray(I["L0_attn_" + w], F32n))
    out["projT"] = np.ascontiguousarray(np.asarray(I["L0_attn_proj"], F32n).T)
    out["bo_row"] = np.ascontiguousarray(np.asarray(I["L0_attn_bo"], F32n)[None, :])
    cst = np.zeros((128, 16), F32n); cst[:, 0] = 1.0
    out["cst"] = cst
    cr = np.zeros((4, 128), F32n); cr[0] = 1.0
    out["cstrow"] = cr
    out["ones512"] = np.ones((1, 512), F32n)
    out["ident"] = np.eye(128, dtype=F32n)
    ep = np.zeros((128, 8, 16), F32n)
    for t_ in range(8):
        for p in range(128):
            ep[p, t_, 2 * t_ + p // 64] = 1.0
    out["epat"] = np.ascontiguousarray(ep.reshape(128, 128))
    out["masks"] = band_masks_np()
    out["wqkv_l"] = np.ascontiguousarray(np.asarray(I["L0_ffn_Wqkv"], F32n))
    out["wo_l"] = np.ascontiguousarray(np.asarray(I["L0_ffn_Wo"], F32n))
    out["w1_l"] = np.ascontiguousarray(np.asarray(I["L0_ffn_W1"], F32n))
    out["w2_l"] = np.ascontiguousarray(np.asarray(I["L0_ffn_W2"], F32n))
    qs = np.asarray(I["L0_ffn_q_scale"], F32n)
    ks = np.asarray(I["L0_ffn_k_scale"], F32n)
    out["qsrows"] = np.ascontiguousarray(np.concatenate(
        [qs, np.concatenate([qs[:, DH // 2:], qs[:, :DH // 2]], 1)], 1)[:, None, :])
    out["ksrows"] = np.ascontiguousarray(np.concatenate(
        [ks, np.concatenate([ks[:, DH // 2:], ks[:, :DH // 2]], 1)], 1)[:, None, :])
    lng = np.asarray(I["L0_ffn_ln_g"], F32n).reshape(DEPTH, 8, 128)
    lnb = np.asarray(I["L0_ffn_ln_b"], F32n).reshape(DEPTH, 8, 128)
    out["lng_l"] = np.ascontiguousarray(lng.transpose(0, 2, 1))
    out["lnb_l"] = np.ascontiguousarray(lnb.transpose(0, 2, 1))
    b1 = np.asarray(I["L0_ffn_b1"], F32n)
    b1a = np.zeros((DEPTH, 22 * 128), F32n); b1a[:, :DI] = b1[:, :DI]
    b1g = np.zeros((DEPTH, 22 * 128), F32n); b1g[:, :DI] = b1[:, DI:]
    out["b1a_l"] = np.ascontiguousarray(b1a.reshape(DEPTH, 22, 128).transpose(0, 2, 1))
    out["b1g_l"] = np.ascontiguousarray(b1g.reshape(DEPTH, 22, 128).transpose(0, 2, 1))
    b2 = np.asarray(I["L0_ffn_b2"], F32n).reshape(DEPTH, 8, 128)
    out["b2_l"] = np.ascontiguousarray(b2.transpose(0, 2, 1))
    rq, rk = rotary_tables_np()
    out["rot_q"], out["rot_k"] = rq, rk
    return out


def make_in_maps(I):
    shared = prep_shared_inputs(I)
    x = np.asarray(I["L0_attn_x"], np.float32)
    maps = []
    for b in range(8):
        m = dict(shared)
        m["xfm"] = np.ascontiguousarray(x[b].T)
        maps.append(m)
    return maps


from concourse.bass_utils import run_bass_kernel_spmd


def kernel(**inputs):
    kb = KB()
    kb.declare()
    kb.p0()
    kb.p1()
    kb.p2_sweep()
    kb.p2_q()
    kb.p2_kv()
    kb.p2_out()
    for li in range(DEPTH):
        kb.p3_attn(li)
        kb.p3_ffn(li)
    kb.p4()
    nc = kb.finish()
    in_maps = make_in_maps(inputs)
    res = run_bass_kernel_spmd(nc, in_maps, core_ids=list(range(8)))
    out = np.empty((8, 2048, 1024), np.float32)
    for b in range(8):
        o = np.asarray(res.results[b]["out_fm"])
        out[b] = o.reshape(128, NF, NTOK).transpose(1, 0, 2).reshape(1024, NTOK).T
    return out


# version 03 (final)
# speedup vs baseline: 1.2041x; 1.2041x over previous
import numpy as np
import ml_dtypes
from contextlib import ExitStack

import concourse.bass as bass
import concourse.mybir as mybir
from concourse.alu_op_type import AluOpType as ALU

F32 = mybir.dt.float32
BF16 = mybir.dt.bfloat16
AF = mybir.ActivationFunctionType
AX = mybir.AxisListType.X

ENGINES = ("tensor", "vector", "scalar", "gpsimd", "sync")

NTOK = 2048
NCH = 4
NF = 8
NH, DH, MF = 16, 64, 256
RATIO = float(MF) ** -0.5
EPS_K = 1e-4
DNRM = float(DH) ** -0.25
WIN = 256
NW = NTOK // WIN
DI = 2730
NJC = 22
JCW = [128] * 21 + [42]
DEPTH = 6


def apbc(ap2d, dims):
    return bass.AP(ap2d.tensor, ap2d.offset, [list(ap2d.ap[0])] + [list(x) for x in dims])


class Op:
    __slots__ = ("eng", "fn", "waits", "inc", "val", "tag", "sem", "line")

    def __init__(self, eng, fn, inc, tag):
        self.eng, self.fn, self.inc, self.tag = eng, fn, inc, tag
        self.waits = {}
        self.sem = eng
        import sys
        f = sys._getframe(2)
        while f is not None and f.f_code.co_name in ("op", "mm", "vop", "sop",
                "dma", "cast_dma", "__init__", "zero_close", "bank_clear",
                "load_weight", "claim"):
            f = f.f_back
        self.line = f.f_lineno if f is not None else 0


DMA_POOL = 8

class Prog:

    def __init__(self):
        self.ops = {e: [] for e in ENGINES}
        self.count = {e: 0 for e in ENGINES}
        self.dma_count = {}
        self.dma_idx = {e: 0 for e in ENGINES}
        self.last_w = {}
        self.readers = {}

    def _record(self, o, eng, reads, writes, accum, tok):
        self.ops[eng].append(o)
        for k in reads:
            self.readers.setdefault(k, []).append(tok)
        for k in writes:
            self.last_w[k] = tok
            if not accum:
                self.readers[k] = []

    def _deps(self, o, reads, writes, accum):
        deps = []
        for k in reads:
            tk = self.last_w.get(k)
            if tk is not None:
                deps.append(tk)
        for k in writes:
            tk = self.last_w.get(k)
            if tk is not None and not accum:
                deps.append(tk)
            for t2 in self.readers.get(k, ()):
                deps.append(t2)
        for (sname, v) in deps:
            o.waits[sname] = max(o.waits.get(sname, 0), v)

    def op(self, eng, fn, reads=(), writes=(), dma=False, accum=False, tag=""):
        if dma:
            i = self.dma_idx[eng] % DMA_POOL
            self.dma_idx[eng] += 1
            sname = f"dma_{eng}_{i}"
            prev = self.dma_count.get(sname, 0)
            o = Op(eng, fn, 16, tag)
            o.sem = sname
            self._deps(o, reads, writes, accum)
            if prev > 0:
                o.waits[sname] = max(o.waits.get(sname, 0), prev)
            self.dma_count[sname] = prev + 16
            o.val = prev + 16
            tok = (sname, o.val)
            self._record(o, eng, reads, writes, accum, tok)
            return tok
        o = Op(eng, fn, 1, tag)
        o.sem = eng
        self._deps(o, reads, writes, accum)
        self.count[eng] += 1
        o.val = self.count[eng]
        tok = (eng, o.val)
        self._record(o, eng, reads, writes, accum, tok)
        return tok

    def barrier_token(self):
        d = {e: self.count[e] for e in ENGINES if self.count[e] > 0}
        d.update(self.dma_count)
        return d

    def wait_all(self, eng, tokens, tag="bar"):
        o = Op(eng, lambda e: e.nop(), 1, tag)
        o.sem = eng
        o.waits = {sn: v for sn, v in tokens.items() if v > 0 and sn != eng}
        self.count[eng] += 1
        o.val = self.count[eng]
        self.ops[eng].append(o)
        return (eng, o.val)

    def n_ops(self):
        return sum(len(v) for v in self.ops.values())

    def emit(self, nc, sems):
        waited = {e: {} for e in ENGINES}

        def emit_engine(proxy, eng):
            w = waited[eng]
            for o in self.ops[eng]:
                for sname, v in sorted(o.waits.items()):
                    if v > w.get(sname, 0):
                        proxy.wait_ge(sems[sname], v)
                        w[sname] = v
                try:
                    o.fn(proxy).then_inc(sems[o.sem], o.inc)
                except Exception:
                    import sys
                    print(f"EMIT FAIL eng={eng} idx={o.val} tag={o.tag}", file=sys.stderr)
                    raise

        with nc.Block() as block:
            @block.tensor
            def _(t):
                emit_engine(t, "tensor")

            @block.vector
            def _(v):
                emit_engine(v, "vector")

            @block.scalar
            def _(s):
                emit_engine(s, "scalar")

            @block.gpsimd
            def _(g):
                emit_engine(g, "gpsimd")

            @block.sync
            def _(s):
                emit_engine(s, "sync")


class KB:
    def __init__(self, n_layers=DEPTH, dbg=()):
        self.nc = bass.Bass("TRN2", target_bir_lowering=False)
        self.P = Prog()
        self.es = ExitStack()
        self.n_layers = n_layers
        self.dbg = set(dbg)
        self.dram = {}
        self.t = {}

    def sb(self, name, shape, dt):
        h = self.es.enter_context(self.nc.sbuf_tensor(name, shape, dt))
        self.t[name] = h
        return h

    def ps(self, name, shape, dt=F32):
        h = self.es.enter_context(self.nc.psum_tensor(name, shape, dt))
        self.t[name] = h
        return h

    def din(self, name, shape, dt=F32):
        h = self.nc.dram_tensor(name, shape, dt, kind="ExternalInput")
        self.dram[name] = h
        return h

    def dout(self, name, shape, dt=F32):
        h = self.nc.dram_tensor(name, shape, dt, kind="ExternalOutput")
        self.dram[name] = h
        return h

    def dint(self, name, shape, dt=F32):
        h = self.nc.dram_tensor(name, shape, dt, kind="Internal")
        self.dram[name] = h
        return h

    def declare(self):
        d = self.din
        d("xfm", [1024, NTOK], F32)
        d("xall", [8, 1024, NTOK], F32)
        for w in ("wq", "wk", "wv", "wo"):
            d(w, [1024, 1024], F32)
        d("projT", [64, MF], F32)
        d("bo_row", [1, 1024], F32)
        d("cst", [128, 16], F32)
        d("cstrow", [4, 128], F32)
        d("ones512", [1, 512], F32)
        d("ident", [128, 128], F32)
        d("epat", [128, 8 * 16], F32)
        d("masks", [128, 3 * 384], F32)
        d("wqkv_l", [DEPTH, 1024, 3072], F32)
        d("wo_l", [DEPTH, 1024, 1024], F32)
        d("w1_l", [DEPTH, 1024, 5460], F32)
        d("w2_l", [DEPTH, DI, 1024], F32)
        d("qsrows", [DEPTH, 1, 128], F32)
        d("ksrows", [DEPTH, 1, 128], F32)
        d("lng_l", [DEPTH, 128, 8], F32)
        d("lnb_l", [DEPTH, 128, 8], F32)
        d("b1a_l", [DEPTH, 128, 22], F32)
        d("b1g_l", [DEPTH, 128, 22], F32)
        d("b2_l", [DEPTH, 128, 8], F32)
        d("rot_q", [NTOK, 128], F32)
        d("rot_k", [NTOK, 128], F32)
        self.dout("out_fm", [128, NF * NTOK], F32)
        self.dint("qmm_sp", [NH, 2, 128, NTOK], BF16)
        self.dint("xacc_sp", [128, NF * NTOK], F32)

        sb, ps = self.sb, self.ps
        sb("ySB", [128, NF * NTOK], F32)
        sb("BIGA", [128, 4096], F32)
        sb("BIGC", [128, 5248], F32)
        sb("BIGB", [128, 24576], BF16)
        sb("WA", [128, 8192], BF16)
        sb("WB", [128, 8192], BF16)
        sb("cstf", [128, 16], F32)
        sb("onescol_bf", [128, 1], BF16)
        sb("onesrow_f", [1, 128], F32)
        sb("onesrow_bf", [1, 128], BF16)
        sb("ones512_bf", [1, 512], BF16)
        sb("ident_bf", [128, 128], BF16)
        sb("ident_f", [128, 128], F32)
        sb("epat_bf", [128, 8 * 16], BF16)
        sb("masks_bf", [128, 3 * 384], BF16)
        sb("projTdn", [128, MF], BF16)
        sb("bo_bf", [1, 1024], BF16)
        sb("murow", [1, 512], F32)
        sb("rstdrow", [1, 512], F32)
        sb("negmubf", [1, 512], BF16)
        sb("r512a", [1, 512], F32)
        sb("r512b", [1, 512], F32)
        sb("rstd_tok", [128, 4], F32)
        sb("gsb_t", [1, 4096], BF16)
        sb("runmax", [128, 1], F32)
        sb("redcol", [128, 1], F32)
        sb("cwkcol", [128, 8], BF16)
        sb("stabcol", [128, 1], F32)
        sb("sq16", [128, 512], BF16)
        sb("diag_tok", [128, NH], F32)
        sb("negb", [128, NH], F32)
        sb("sumsq", [128, NH], F32)
        sb("invv", [128, NH], F32)
        sb("recipd", [128, NH], F32)
        sb("denom", [128, NH], F32)
        sb("lng_t", [128, 8], F32)
        sb("lnb_t", [128, 8], F32)
        sb("b1a_t", [128, 22], F32)
        sb("b1g_t", [128, 22], F32)
        sb("b2_t", [128, 8], F32)
        sb("edgec", [128, 28], F32)
        sb("mx2", [128, 2], F32)
        sb("dn2", [128, 2], F32)
        sb("rc2", [128, 2], F32)
        ps("ps_a", [128, 2048], F32)
        ps("ps_m0", [128, 512], F32)
        ps("ps_m1", [128, 512], F32)
        ps("ps_row", [128, 512], F32)
        ps("ps_tb", [128, 1024], BF16)

        self.sems = {}
        for e in ENGINES:
            self.sems[e] = self.es.enter_context(self.nc.semaphore("sem_" + e))
        for e in ("sync", "gpsimd", "scalar", "vector", "tensor"):
            for i in range(8):
                self.sems[f"dma_{e}_{i}"] = self.es.enter_context(
                    self.nc.semaphore(f"sdma_{e}_{i}"))

        B = self.t["BIGB"]
        self.xnbf = B[:, 0:16384]
        self.ptok = B[:, 16384:20480]
        self.chunk = B[:, 20480:24576]
        self.attcat = B[:, 0:16384]
        self.qmmt = B[:, 16384:20480]
        self.wqkv = B[:, 0:24576]
        self.xn2g = B[:, 0:4096]
        self.hgrp = B[:, 4096:15360]
        self.xn2gB = B[:, 15360:19456]
        A = self.t["BIGA"]
        self.scr32 = A[:, 0:1024]
        self.dinvf = A[:, 1024:3072]
        self.xt = A[:, 3072:3584]
        self.xt2 = A[:, 3584:4096]
        C = self.t["BIGC"]
        CB = C[:, :].bitcast(BF16)
        self.vext = CB[:, 0:1040]
        self.ctxsb = C[:, 520:2600]
        self.expraw = C[:, 2600:3112]
        self.qmmstg = CB[:, 6224:7248]
        self.ctxbf = CB[:, 7248:9328]
        self.gsb = self.t["gsb_t"][0:1, :]
        self.qsbc = C[:, 4664:4792]
        self.ksbc = C[:, 4792:4920]
        self.zerorow = CB[0:1, 9840:9968]
        self.qs_f = C[0:1, 4984:5112]
        self.projT_f = C[:, 0:256]
        self.ks_f = C[0:1, 5112:5240]
        self.ysw = CB[:, 0:2048]
        self.qrtok = CB[:, 2048:3072]
        self.qF = CB[:, 3072:5120]
        self.attnw = CB[:, 5120:7168]
        self.rotw = A[:, 3072:3328]
        self.rots = A[:, 3328:3584]
        self.expn = CB[:, 8192:8576]
        self.expTn = CB[:, 8576:8960]
        self.expnB = CB[:, 7168:7552]
        self.expTnB = CB[:, 7552:7936]
        self.gelu16 = CB[:, 8960:9472]
        self.ttok1 = A[:, 0:1024]
        self.ttok2 = A[:, 1024:2048]
        self.ttok3 = A[:, 2048:3072]
        W = self.t["WB"]
        self.kF = W[:, 0:4096]
        self.vT = W[:, 4096:8192]

    def mm(self, out_ap, lhsT, rhs, reads, writes, start=True, stop=True,
           accum=False, tp=None, tag=""):
        self.P.op("tensor",
                  lambda e: e.matmul(out_ap, lhsT, rhs, start=start, stop=stop,
                                     tile_position=tp),
                  reads=reads, writes=writes, accum=accum, tag=tag)

    def bank_clear(self, psname, col, reads=(), tag="bclr"):
        t = self.t
        self.mm(t[psname][0:1, col:col + 1], t["onesrow_f"][0:1, 0:1],
                t["ident_f"][0:1, 0:1], list(reads) + ["ones_r", "ident_f"],
                [psname], start=True, stop=True, tag=tag)

    def claim(self, keys):
        self.P.op("vector", lambda e: e.memset(self.t["redcol"][:, :], 0.0),
                  reads=(), writes=list(keys) + ["redcol"], tag="claim")

    def zero_close(self, out_ap, okey, ncols):
        t = self.t
        self.mm(out_ap, self.zerorow, t["projTdn"][0:1, 0:ncols],
                ["zero_rb", "projTdn"], [okey], start=False, stop=True, accum=True)

    def cast_dma(self, out_ap, in_ap, reads, writes, tag=""):
        self.P.op("gpsimd", lambda e: e.dma_start(out=out_ap, in_=in_ap),
                  reads=reads, writes=writes, dma=True, tag=tag)

    def dma(self, out_ap, in_ap, reads, writes, tag="", eng="sync"):
        self.P.op(eng, lambda e: e.dma_start(out=out_ap, in_=in_ap),
                  reads=reads, writes=writes, dma=True, tag=tag)

    def vop(self, fn, reads, writes, tag=""):
        self.P.op("vector", fn, reads=reads, writes=writes, tag=tag)

    def sop(self, fn, reads, writes, tag=""):
        self.P.op("scalar", fn, reads=reads, writes=writes, tag=tag)

    def load_weight(self, dst_sb, dram_ap_2d, din_, reads, writes):
        k = din_ // 128
        self.cast_dma(dst_sb.rearrange("p (k m) -> p k m", k=k),
                      dram_ap_2d.rearrange("(k p) m -> p k m", p=128),
                      reads, writes)

    def transpose_col(self, src_row_ap, reads, dst_col_ap, wkey, pscol=0):
        t = self.t
        self.mm(t["ps_m1"][:, pscol:pscol + 1], src_row_ap, t["ident_f"][0:1, 0:1],
                list(reads) + ["ident_f"], ["ps_m1"])
        self.sop(lambda e: e.copy(dst_col_ap, t["ps_m1"][:, pscol:pscol + 1]),
                 ["ps_m1"], [wkey])

    def p0(self):
        t, d = self.t, self.dram
        self.dma(t["cstf"][:, :], d["cst"][:, :], [], ["cst"])
        self.dma(t["onesrow_f"][:, :], d["cstrow"][0:1, :], [], ["ones_r"])
        self.cast_dma(t["onesrow_bf"][:, :], d["cstrow"][0:1, :], [], ["ones_rb"])
        self.cast_dma(self.zerorow, d["cstrow"][1:2, :], [], ["zero_rb"])
        self.cast_dma(t["ones512_bf"][:, :], d["ones512"][:, :], [], ["ones512"])
        self.sop(lambda e: e.copy(t["onescol_bf"][:, :], t["cstf"][:, 0:1]),
                 ["cst"], ["ones_c"])
        self.cast_dma(t["ident_bf"][:, :], d["ident"][:, :], [], ["ident_b"])
        self.dma(t["ident_f"][:, :], d["ident"][:, :], [], ["ident_f"])
        self.cast_dma(t["epat_bf"][:, :], d["epat"][:, :], [], ["epat"])
        self.cast_dma(t["masks_bf"][:, :], d["masks"][:, :], [], ["masks"])
        self.dma(self.projT_f[0:64, :], d["projT"][:, :], [], ["projT_f"])
        self.dma(self.projT_f[64:128, :], d["projT"][:, :], ["projT_f"], ["projT_f"],
                 eng="sync")
        self.cast_dma(t["bo_bf"][:, :], d["bo_row"][:, :], [], ["bo_bf"])
        self.sop(lambda e: e.activation(t["projTdn"][:, :], self.projT_f[:, :],
                                        AF.Copy, scale=DNRM),
                 ["projT_f"], ["projTdn"])
        self.cast_dma(self.xnbf.rearrange("p (f n) -> p f n", f=NF),
                      d["xfm"].ap().rearrange("(f p) n -> p f n", p=128), [], ["xnbf"])

    def stats_chunk(self, src, src_key, c, do_negmu=False, tok_stride=NTOK):
        t = self.t
        csl = slice(0, 512)
        for f in range(NF):
            fs = slice(f * tok_stride + c * 512, f * tok_stride + (c + 1) * 512)
            self.mm(t["ps_row"][0:1, :], t["onescol_bf"][:, :], src[:, fs],
                    ["ones_c", src_key], ["ps_row"], start=(f == 0), stop=(f == NF - 1),
                    accum=(f > 0))
        self.vop(lambda e: e.tensor_scalar(t["murow"][0:1, csl], t["ps_row"][0:1, :],
                                           1.0 / 1024.0, None, ALU.mult),
                 ["ps_row"], ["murow"])
        for f in range(NF):
            fs = slice(f * tok_stride + c * 512, f * tok_stride + (c + 1) * 512)
            self.sop(lambda e, fs=fs: e.activation(t["sq16"][:, :], src[:, fs], AF.Square),
                     [src_key], ["sq16"])
            self.mm(t["ps_row"][0:1, :], t["onescol_bf"][:, :], t["sq16"][:, :],
                    ["ones_c", "sq16"], ["ps_row"], start=(f == 0), stop=(f == NF - 1),
                    accum=(f > 0))
        self.vop(lambda e: e.tensor_scalar(t["r512a"][:, :], t["ps_row"][0:1, :],
                                           1.0 / 1024.0, None, ALU.mult),
                 ["ps_row"], ["r512a"])
        self.vop(lambda e: e.tensor_tensor(t["r512b"][:, :], t["murow"][0:1, csl],
                                           t["murow"][0:1, csl], ALU.mult),
                 ["murow"], ["r512b"])
        self.vop(lambda e: e.tensor_tensor(t["r512a"][:, :], t["r512a"][:, :],
                                           t["r512b"][:, :], ALU.subtract),
                 ["r512a", "r512b"], ["r512a"])
        self.vop(lambda e: e.tensor_scalar(t["r512a"][:, :], t["r512a"][:, :],
                                           1e-5, None, ALU.add),
                 ["r512a"], ["r512a"])
        self.sop(lambda e: e.activation(t["r512a"][:, :], t["r512a"][:, :], AF.Ln),
                 ["r512a"], ["r512a"])
        self.sop(lambda e: e.activation(t["rstdrow"][0:1, csl], t["r512a"][:, :],
                                        AF.Exp, scale=-0.5),
                 ["r512a"], ["rstdrow"])
        if do_negmu:
            self.vop(lambda e: e.tensor_scalar(t["negmubf"][0:1, csl], t["murow"][0:1, csl],
                                               -1.0, None, ALU.mult),
                     ["murow"], ["negmubf"])

    def p1(self):
        t = self.t
        for c in range(NCH):
            self.stats_chunk(self.xnbf, "xnbf", c)
            self.mm(t["ps_m0"][:, :], t["onesrow_f"][:, :], t["murow"][0:1, :],
                    ["murow", "ones_r"], ["ps_m0"])
            self.mm(t["ps_m1"][:, :], t["onesrow_f"][:, :], t["rstdrow"][0:1, :],
                    ["rstdrow", "ones_r"], ["ps_m1"])
            for f in range(NF):
                fs = slice(f * NTOK + c * 512, f * NTOK + (c + 1) * 512)
                self.dma(self.xt[:, :],
                         self.dram["xfm"].ap()[f * 128:(f + 1) * 128,
                                               c * 512:(c + 1) * 512],
                         [], ["xt"], eng="sync")
                self.vop(lambda e, fs=fs: e.tensor_tensor(
                    self.scr32[:, 0:512], self.xt[:, :], t["ps_m0"][:, :], ALU.subtract),
                    ["xt", "ps_m0"], ["scrA"])
                self.vop(lambda e, fs=fs: e.tensor_tensor(
                    t["ySB"][:, fs], self.scr32[:, 0:512], t["ps_m1"][:, :], ALU.mult),
                    ["scrA", "ps_m1"], ["ySB"])
        for f in range(NF):
            for c in range(NCH):
                fs = slice(f * NTOK + c * 512, f * NTOK + (c + 1) * 512)
                self.sop(lambda e, fs=fs: e.copy(self.xnbf[:, fs], t["ySB"][:, fs]),
                         ["ySB"], ["xnbf"])

    def p2_sweep(self):
        t, d = self.t, self.dram
        self.load_weight(t["WB"][:, :], d["wk"].ap(), 1024, [], ["WB"])
        for half in range(2):
            for kc in range(8):
                self.mm(t["ps_row"][0:1, :], t["onescol_bf"][:, :],
                        t["WB"][:, kc * 1024 + half * 512: kc * 1024 + half * 512 + 512],
                        ["ones_c", "WB"], ["ps_row"], start=(kc == 0), stop=(kc == 7),
                        accum=(kc > 0))
            dst, key = (t["r512a"], "r512a") if half == 0 else (t["r512b"], "r512b")
            self.vop(lambda e, dst=dst: e.tensor_copy(dst[:, :], t["ps_row"][0:1, :]),
                     ["ps_row"], [key])
        for j in range(8):
            src, skey = (t["r512a"], "r512a") if j < 4 else (t["r512b"], "r512b")
            off = (j % 4) * 128
            self.transpose_col(src[0:1, off:off + 128], [skey],
                               t["cwkcol"][:, j:j + 1], "cwkcol")
        for h in range(NH):
            base, col = (h % 2) * 64, h // 2
            self.mm(t["ps_row"][0:1, 0:MF], t["cwkcol"][base:base + 64, col:col + 1],
                    t["projTdn"][base:base + 64, :], ["cwkcol", "projTdn"], ["ps_row"])
            self.sop(lambda e, h=h: e.copy(self.gsb[0:1, h * MF:(h + 1) * MF],
                                           t["ps_row"][0:1, 0:MF]),
                     ["ps_row"], ["gsb"])
        self.vop(lambda e: e.memset(t["runmax"][:, :], -3.0e38), [], ["runmax"])
        for ob in range(8):
            for c in range(NCH):
                self.cast_dma(self.chunk.rearrange("p (f n) -> p f n", f=NF),
                              d["xall"][ob, :, c * 512:(c + 1) * 512]
                              .rearrange("(f p) n -> p f n", p=128),
                              ["chunk"], ["chunk"])
                self.stats_chunk(self.chunk, "chunk", 0, do_negmu=True, tok_stride=512)
                for i in range(4):
                    self.transpose_col(
                        t["rstdrow"][0:1, i * 128:(i + 1) * 128],
                        ["rstdrow"], t["rstd_tok"][:, i:i + 1], "rstd_tok")
                for mc in range(8):
                    pm, pk = (t["ps_m0"], "ps_m0") if mc % 2 == 0 else (t["ps_m1"], "ps_m1")
                    for kc in range(8):
                        self.mm(pm[:, :],
                                t["WB"][:, kc * 1024 + mc * 128: kc * 1024 + mc * 128 + 128],
                                self.chunk[:, kc * 512:(kc + 1) * 512],
                                ["WB", "chunk"], [pk], start=(kc == 0), stop=(kc == 7),
                                accum=(kc > 0))
                    self.sop(lambda e, mc=mc, pm=pm: e.copy(
                        self.ptok[:, mc * 512:(mc + 1) * 512], pm[:, :]),
                        [pk], ["ptok"])
                for i in range(4):
                    ns = slice(i * 128, (i + 1) * 128)
                    for hg in range(2):
                        for hh in range(8):
                            h = hg * 8 + hh
                            base, ft = (h % 2) * 64, h // 2
                            out = t["ps_a"][:, hh * MF:(hh + 1) * MF]
                            self.mm(out,
                                    self.ptok[base:base + 64,
                                              ft * 512 + i * 128: ft * 512 + (i + 1) * 128],
                                    t["projTdn"][base:base + 64, :],
                                    ["ptok", "projTdn"], ["ps_a"], start=True, stop=False,
                                    accum=(hh > 0))
                            self.mm(out, t["negmubf"][0:1, ns],
                                    self.gsb[0:1, h * MF:(h + 1) * MF],
                                    ["negmubf", "gsb"], ["ps_a"], start=False, stop=True,
                                    accum=True)
                        self.vop(lambda e: e.tensor_reduce(
                            t["redcol"][:, :], t["ps_a"][:, :], AX, ALU.max),
                            ["ps_a"], ["redcol"])
                        self.vop(lambda e, i=i: e.tensor_tensor(
                            t["redcol"][:, :], t["redcol"][:, :], t["rstd_tok"][:, i:i + 1],
                            ALU.mult),
                            ["redcol", "rstd_tok"], ["redcol"])
                        self.vop(lambda e: e.tensor_tensor(
                            t["runmax"][:, :], t["runmax"][:, :], t["redcol"][:, :],
                            ALU.max),
                            ["redcol", "runmax"], ["runmax"])
        self.mm(t["ps_row"][0:1, 0:128], t["runmax"][:, 0:1], t["ident_f"][:, :],
                ["runmax", "ident_f"], ["ps_row"])
        self.vop(lambda e: e.tensor_reduce(t["r512a"][0:1, 0:1], t["ps_row"][0:1, 0:128],
                                           AX, ALU.max),
                 ["ps_row"], ["r512a"])
        self.mm(t["ps_m0"][:, 0:1], t["onesrow_f"][:, :], t["r512a"][0:1, 0:1],
                ["r512a", "ones_r"], ["ps_m0"])
        self.vop(lambda e: e.tensor_copy(t["stabcol"][:, :], t["ps_m0"][:, 0:1]),
                 ["ps_m0"], ["stabcol"])

    def qk_proj_chunk(self, wbuf, wkey, c):
        t = self.t
        for mc in range(8):
            pm, pk = (t["ps_m0"], "ps_m0") if mc % 2 == 0 else (t["ps_m1"], "ps_m1")
            for kc in range(8):
                self.mm(pm[:, :],
                        wbuf[:, kc * 1024 + mc * 128: kc * 1024 + mc * 128 + 128],
                        self.xnbf[:, kc * NTOK + c * 512: kc * NTOK + (c + 1) * 512],
                        [wkey, "xnbf"], [pk], start=(kc == 0), stop=(kc == 7),
                        accum=(kc > 0))
            self.sop(lambda e, mc=mc, pm=pm: e.copy(
                self.chunk[:, mc * 512:(mc + 1) * 512], pm[:, :]),
                [pk], ["chunk"])

    def diag_tok128(self, c, i):
        t = self.t
        if i == 0:
            for f in range(NF):
                self.sop(lambda e, f=f: e.activation(
                    t["sq16"][:, :], self.chunk[:, f * 512:(f + 1) * 512], AF.Square),
                    ["chunk"], ["sq16"])
                self.mm(t["ps_m0"][0:16, :], t["epat_bf"][:, f * 16:(f + 1) * 16],
                        t["sq16"][:, :], ["epat", "sq16"], ["ps_m0"],
                        start=(f == 0), stop=(f == NF - 1), accum=(f > 0))
            self.sop(lambda e: e.copy(self.scr32[0:16, 0:512], t["ps_m0"][0:16, :]),
                     ["ps_m0"], ["scrA"])
        self.mm(t["ps_m1"][:, 0:16], self.scr32[0:16, i * 128:(i + 1) * 128],
                t["ident_f"][0:16, 0:16], ["scrA", "ident_f"], ["ps_m1"])
        self.sop(lambda e: e.activation(t["diag_tok"][:, :], t["ps_m1"][:, 0:16],
                                        AF.Copy, scale=0.5 * DNRM * DNRM),
                 ["ps_m1"], ["diag_tok"])

    def p2_q(self):
        t, d = self.t, self.dram
        self.load_weight(t["WA"][:, :], d["wq"].ap(), 1024, ["WA"], ["WA"])
        for c in range(NCH):
            self.qk_proj_chunk(t["WA"], "WA", c)
            for i in range(4):
                self.diag_tok128(c, i)
                for hg in range(2):
                    for hh in range(8):
                        h = hg * 8 + hh
                        base, ft = (h % 2) * 64, h // 2
                        self.mm(t["ps_a"][:, hh * MF:(hh + 1) * MF],
                                self.chunk[base:base + 64,
                                           ft * 512 + i * 128: ft * 512 + (i + 1) * 128],
                                t["projTdn"][base:base + 64, :],
                                ["chunk", "projTdn"], ["ps_a"], accum=(hh > 0),
                                start=True, stop=False)
                        self.zero_close(t["ps_a"][:, hh * MF:(hh + 1) * MF], "ps_a", MF)
                    self.vop(lambda e, hg=hg: e.tensor_reduce(
                        t["negb"][:, hg * 8:hg * 8 + 8],
                        t["ps_a"][:, :].rearrange("p (h m) -> p h m", h=8),
                        AX, ALU.max, negate=True),
                        ["ps_a"], ["negb"])
                    self.vop(lambda e, hg=hg: e.tensor_tensor(
                        t["negb"][:, hg * 8:hg * 8 + 8],
                        t["negb"][:, hg * 8:hg * 8 + 8],
                        t["diag_tok"][:, hg * 8:hg * 8 + 8], ALU.subtract),
                        ["negb", "diag_tok"], ["negb"])
                    for hh in range(8):
                        h = hg * 8 + hh
                        self.sop(lambda e, hh=hh, h=h: e.activation(
                            self.expraw[:, 0:MF], t["ps_a"][:, hh * MF:(hh + 1) * MF],
                            AF.Exp, bias=t["negb"][:, h:h + 1]),
                            ["ps_a", "negb"], ["expraw"])
                        self.vop(lambda e, h=h: e.tensor_scalar(
                            self.ptok[:, h * MF:(h + 1) * MF], self.expraw[:, 0:MF],
                            RATIO, RATIO * EPS_K, ALU.mult, ALU.add),
                            ["expraw"], ["ptok"])
                tok = c * 4 + i
                for r in range(4):
                    mc, h0 = r // 2, (r % 2) * 8
                    for idx in range(8):
                        h = h0 + idx
                        self.P.op("tensor", lambda e, h=h, mc=mc, idx=idx: e.transpose(
                            t["ps_tb"][:, idx * 128:(idx + 1) * 128],
                            self.ptok[:, h * MF + mc * 128: h * MF + (mc + 1) * 128],
                            t["ident_bf"][:, :]),
                            reads=["ptok", "ident_b"], writes=["ps_tb"],
                            accum=(idx > 0), tag="qT")
                    self.sop(lambda e: e.copy(self.qmm_stage(), t["ps_tb"][:, :]),
                             ["ps_tb"], ["ptok_stage"])
                    self.dma(d["qmm_sp"].ap()[h0:h0 + 8, mc, :, tok * 128:(tok + 1) * 128]
                             .rearrange("h p n -> p h n"),
                             self.qmm_stage().rearrange("p (h n) -> p h n", h=8),
                             ["ptok_stage"], ["qmm_sp"], eng="sync")

    def qmm_stage(self):
        return self.qmmstg

    def ctx_slot(self, s):
        t = self.t
        if s < 28:
            return (t["ps_a"][:, (s // 7) * 512 + (s % 7) * 65:
                              (s // 7) * 512 + (s % 7) * 65 + 65], "ps_a")
        return (t["ps_row"][:, (s - 28) * 65:(s - 28) * 65 + 65], "ps_row")

    def p2_kv(self):
        t, d = self.t, self.dram
        self.load_weight(t["WA"][:, :], d["wv"].ap(), 1024, ["WA"], ["WA"])
        for h in range(NH):
            self.vop(lambda e, h=h: e.memset(self.vext[:, h * 65 + 64: h * 65 + 65], 1.0),
                     [], ["vext"])
        self.vop(lambda e: e.memset(self.ctxsb[:, :], 0.0), [], ["ctx_sb"])
        for c in range(NCH):
            self.qk_proj_chunk(t["WB"], "WB", c)
            for i in range(4):
                tok = c * 4 + i
                self.diag_tok128(c, i)
                self.vop(lambda e: e.tensor_scalar(
                    t["negb"][:, :], t["diag_tok"][:, :], t["stabcol"][:, 0:1], -1.0,
                    ALU.add, ALU.mult),
                    ["diag_tok", "stabcol"], ["negb"])
                for hg in range(2):
                    for hh in range(8):
                        h = hg * 8 + hh
                        base, ft = (h % 2) * 64, h // 2
                        kout = t["ps_m0"][:, 0:MF] if hh % 2 == 0 else t["ps_m1"][:, 0:MF]
                        kkey = "ps_m0" if hh % 2 == 0 else "ps_m1"
                        self.mm(kout,
                                self.chunk[base:base + 64,
                                           ft * 512 + i * 128: ft * 512 + (i + 1) * 128],
                                t["projTdn"][base:base + 64, :],
                                ["chunk", "projTdn"], [kkey], start=True, stop=False)
                        self.zero_close(kout, kkey, MF)
                        pm, pk = (t["ps_m0"], "ps_m0") if hh % 2 == 0 else (t["ps_m1"], "ps_m1")
                        self.sop(lambda e, h=h, pm=pm: e.activation(
                            self.expraw[:, 0:MF], pm[:, 0:MF],
                            AF.Exp, bias=t["negb"][:, h:h + 1]),
                            [pk, "negb"], ["expraw"])
                        self.vop(lambda e, h=h: e.tensor_scalar(
                            self.ptok[:, h * MF:(h + 1) * MF], self.expraw[:, 0:MF],
                            RATIO, RATIO * EPS_K, ALU.mult, ALU.add),
                            ["expraw"], ["ptok"])
                for half in range(2):
                    pm, pk = (t["ps_m0"], "ps_m0") if half == 0 else (t["ps_m1"], "ps_m1")
                    for kc in range(8):
                        self.mm(pm[:, :],
                                self.xnbf[:, kc * NTOK + tok * 128: kc * NTOK + (tok + 1) * 128],
                                t["WA"][:, kc * 1024 + half * 512: kc * 1024 + half * 512 + 512],
                                ["WA", "xnbf"], [pk], start=(kc == 0), stop=(kc == 7),
                                accum=(kc > 0))
                    for hh in range(8):
                        h = half * 8 + hh
                        self.sop(lambda e, h=h, hh=hh, pm=pm: e.copy(
                            self.vext[:, h * 65: h * 65 + 64], pm[:, hh * 64:(hh + 1) * 64]),
                            [pk], ["vext"])
                for h in range(NH):
                    for mcc in range(2):
                        s = h * 2 + mcc
                        out, okey = self.ctx_slot(s)
                        self.mm(out,
                                self.ptok[:, h * MF + mcc * 128: h * MF + (mcc + 1) * 128],
                                self.vext[:, h * 65:(h + 1) * 65],
                                ["ptok", "vext"], [okey], start=True, stop=True,
                                accum=(s > 0))
                psa_ap = apbc(t["ps_a"][:, 0:455], [[512, 4], [1, 455]])
                csb_ap = apbc(self.ctxsb[:, 0:455], [[455, 4], [1, 455]])
                self.vop(lambda e, psa_ap=psa_ap, csb_ap=csb_ap: e.tensor_tensor(
                    csb_ap, csb_ap, psa_ap, ALU.add),
                    ["ps_a", "ctx_sb"], ["ctx_sb"])
                self.vop(lambda e: e.tensor_tensor(
                    self.ctxsb[:, 1820:2080], self.ctxsb[:, 1820:2080],
                    t["ps_row"][:, 0:260], ALU.add),
                    ["ps_row", "ctx_sb"], ["ctx_sb"])
        self.sop(lambda e: e.copy(self.ctxbf[:, :], self.ctxsb[:, :]),
                 ["ctx_sb"], ["ctx_bf"])

    def p2_out(self):
        t, d = self.t, self.dram
        qa = self.qmmt[:, 0:2048]
        qb = self.qmmt[:, 2048:4096]
        for hp in range(8):
            for sub in range(2):
                h = 2 * hp + sub
                self.dma(qa, d["qmm_sp"].ap()[h, 0, :, :], ["qmm_sp", "ptok"],
                         ["ptok"], eng="sync")
                self.dma(qb, d["qmm_sp"].ap()[h, 1, :, :], ["qmm_sp", "ptok"],
                         ["ptok"], eng="sync")
                for nc_ in range(4):
                    if nc_ < 3:
                        slot, skey = t["ps_m0"][32 * nc_:32 * nc_ + 1, :], "ps_m0"
                    else:
                        slot, skey = t["ps_m1"][0:1, :], "ps_m1"
                    self.mm(slot, self.ctxbf[:, (2 * h) * 65 + 64:(2 * h) * 65 + 65],
                            qa[:, nc_ * 512:(nc_ + 1) * 512],
                            ["ctx_bf", "ptok"], [skey], start=True, stop=False)
                    self.mm(slot, self.ctxbf[:, (2 * h + 1) * 65 + 64:(2 * h + 1) * 65 + 65],
                            qb[:, nc_ * 512:(nc_ + 1) * 512],
                            ["ctx_bf", "ptok"], [skey], start=False, stop=True,
                            accum=True)
                sub1rows = [t["murow"], t["rstdrow"], t["r512a"], t["r512b"]]
                for nc_ in range(4):
                    if nc_ < 3:
                        slot, skey = t["ps_m0"][32 * nc_:32 * nc_ + 1, :], "ps_m0"
                    else:
                        slot, skey = t["ps_m1"][0:1, :], "ps_m1"
                    if sub == 0:
                        dst = self.dinvf[0:1, nc_ * 512:(nc_ + 1) * 512]
                    else:
                        dst = sub1rows[nc_][0:1, :]
                    self.vop(lambda e, slot=slot, dst=dst: e.reciprocal(dst, slot),
                             [skey], ["dinv"])
                for mcc in range(2):
                    qt = qa if mcc == 0 else qb
                    sidx = h * 2 + mcc
                    for nc_ in range(4):
                        self.mm(t["ps_a"][sub * 64:sub * 64 + 64,
                                          nc_ * 512:(nc_ + 1) * 512],
                                self.ctxbf[:, sidx * 65: sidx * 65 + 64],
                                qt[:, nc_ * 512:(nc_ + 1) * 512],
                                ["ctx_bf", "ptok"], ["ps_a"],
                                start=(mcc == 0), stop=(mcc == 1),
                                accum=(mcc > 0), tp=(0, sub * 64))
            sub1rows = [t["murow"], t["rstdrow"], t["r512a"], t["r512b"]]
            for nc_ in range(4):
                self.mm(t["ps_m0"][0:64, :], t["onesrow_f"][0:1, 0:64],
                        self.dinvf[0:1, nc_ * 512:(nc_ + 1) * 512],
                        ["dinv", "ones_r"], ["ps_m0"], tp=(0, 0))
                self.mm(t["ps_m0"][64:128, :], t["onesrow_f"][0:1, 0:64],
                        sub1rows[nc_][0:1, :],
                        ["dinv", "ones_r"], ["ps_m0"], accum=True, tp=(0, 64))
                self.vop(lambda e: e.tensor_copy(self.scr32[:, 0:512], t["ps_m0"][:, :]),
                         ["ps_m0"], ["scrA"])
                self.vop(lambda e, hp=hp, nc_=nc_: e.tensor_tensor(
                    self.attcat[:, hp * NTOK + nc_ * 512: hp * NTOK + (nc_ + 1) * 512],
                    t["ps_a"][:, nc_ * 512:(nc_ + 1) * 512],
                    self.scr32[:, 0:512], ALU.mult),
                    ["ps_a", "scrA"], ["xnbf"])
        self.load_weight(t["WB"][:, :], d["wo"].ap(), 1024, ["WB"], ["WB"])
        for mc in range(8):
            for nc_ in range(4):
                pm, pk = (t["ps_m0"], "ps_m0") if nc_ % 2 == 0 else (t["ps_m1"], "ps_m1")
                for kc in range(8):
                    self.mm(pm[:, :],
                            t["WB"][:, kc * 1024 + mc * 128: kc * 1024 + mc * 128 + 128],
                            self.attcat[:, kc * NTOK + nc_ * 512: kc * NTOK + (nc_ + 1) * 512],
                            ["WB", "xnbf"], [pk], start=(kc == 0), stop=False,
                            accum=(kc > 0))
                self.mm(pm[:, :], t["bo_bf"][0:1, mc * 128:(mc + 1) * 128],
                        t["ones512_bf"][0:1, :],
                        ["bo_bf", "ones512"], [pk], start=False, stop=True, accum=True)
                self.dma(self.xt[:, :],
                         d["xfm"].ap()[mc * 128:(mc + 1) * 128, nc_ * 512:(nc_ + 1) * 512],
                         ["xt"], ["xt"], eng="sync")
                self.vop(lambda e, pm=pm: e.tensor_tensor(
                    self.xt2[:, :], self.xt[:, :], pm[:, :], ALU.add),
                    ["xt", pk], ["xt2"])
                self.dma(d["xacc_sp"].ap()[:, mc * NTOK + nc_ * 512:
                                           mc * NTOK + (nc_ + 1) * 512],
                         self.xt2[:, :], ["xt2"], ["xacc_sp"], eng="sync")

    def p3_attn(self, li):
        t, d = self.t, self.dram
        self.claim(["wqkv", "xnbf", "ptok", "chunk", "xn2g", "hgrp", "WA",
                    "w2mc0", "w2mc1", "WB", "kF", "vT", "w1jc0", "w1jc1",
                    "ysw", "qrtok", "qF", "attnw", "ctx_bf", "vext", "ctx_sb",
                    "expraw", "ptok_stage", "expn", "expTn", "expnB",
                    "expTnB", "gelu16", "rotw", "rots", "xt", "xt2",
                    "ps_a", "ps_aL0", "ps_aL1", "ySB", "ySBg0", "ySBg1",
                    "ySBw0", "ySBw1", "xn2gB"])
        self.load_weight(self.wqkv, d["wqkv_l"].ap()[li], 1024, [], ["wqkv"])
        self.load_weight(t["WA"][:, :], d["wo_l"].ap()[li], 1024, [], ["WA"])
        self.dma(self.qs_f, d["qsrows"].ap()[li], [], ["qs_f"])
        self.dma(self.ks_f, d["ksrows"].ap()[li], [], ["ks_f"])
        for half in range(2):
            self.mm(t["ps_m0"][:, 0:64], t["onesrow_f"][:, :],
                    self.qs_f[0:1, half * 64:(half + 1) * 64],
                    ["qs_f", "ones_r"], ["ps_m0"])
            self.vop(lambda e, half=half: e.tensor_copy(
                self.qsbc[:, half * 64:(half + 1) * 64], t["ps_m0"][:, 0:64]),
                ["ps_m0"], ["qsbc"])
            self.mm(t["ps_m0"][:, 0:64], t["onesrow_f"][:, :],
                    self.ks_f[0:1, half * 64:(half + 1) * 64],
                    ["ks_f", "ones_r"], ["ps_m0"])
            self.vop(lambda e, half=half: e.tensor_copy(
                self.ksbc[:, half * 64:(half + 1) * 64], t["ps_m0"][:, 0:64]),
                ["ps_m0"], ["ksbc"])
        self.vop(lambda e: e.memset(self.kF[:, 2048:4096], 0.0),
                 [], ["kF"])
        self.vop(lambda e: e.memset(self.vT[:, 2048:4096], 0.0), [], ["vT"])
        for f in range(4, NF):
            self.sop(lambda e, f=f: e.copy(
                t["edgec"][:, (f - 4) * 7:(f - 4) * 7 + 7],
                apbc(t["ySB"][:, f * NTOK + WIN - 1: f * NTOK + WIN], [[WIN, 7]])),
                ["ySB"], ["edgec"])

        for w in range(NW):
            cur, prev = w % 2, 1 - (w % 2)
            wsl = slice(w * WIN, (w + 1) * WIN)
            for f in range(NF):
                dst = self.ysw[:, f * 256:(f + 1) * 256]
                if f < 4:
                    self.sop(lambda e, f=f, dst=dst, w=w: e.copy(
                        dst, t["ySB"][:, f * NTOK + w * WIN: f * NTOK + (w + 1) * WIN]),
                        ["ySB"], ["ysw"])
                elif w > 0:
                    self.sop(lambda e, f=f, dst=dst, w=w: e.copy(
                        dst[:, 0:1], t["edgec"][:, (f - 4) * 7 + w - 1:(f - 4) * 7 + w]),
                        ["edgec"], ["ysw"])
                    self.sop(lambda e, f=f, dst=dst, w=w: e.copy(
                        dst[:, 1:256], t["ySB"][:, f * NTOK + w * WIN: f * NTOK + w * WIN + 255]),
                        ["ySB"], ["ysw"])
                else:
                    self.vop(lambda e, dst=dst: e.memset(dst[:, 0:1], 0.0), [], ["ysw"])
                    self.sop(lambda e, f=f, dst=dst: e.copy(
                        dst[:, 1:256], t["ySB"][:, f * NTOK: f * NTOK + 255]),
                        ["ySB"], ["ysw"])
            for t2 in range(2):
                specs = [("q", t["ps_a"], 0, "ps_a"), ("k", t["ps_a"], 1024, "ps_a"),
                         ("v", None, 0, None)]
                for pi, (pname, ptens, poff, pkey) in enumerate(specs):
                    for sub in range(2):
                        if pname == "v":
                            ptens = t["ps_m0"] if sub == 0 else t["ps_m1"]
                            pkey = "ps_m0" if sub == 0 else "ps_m1"
                            out = ptens[:, :]
                        else:
                            out = ptens[:, poff + sub * 512: poff + (sub + 1) * 512]
                        for kc in range(8):
                            self.mm(out,
                                    self.ysw[:, kc * 256 + t2 * 128: kc * 256 + (t2 + 1) * 128],
                                    self.wqkv[:, kc * 3072 + pi * 1024 + sub * 512:
                                              kc * 3072 + pi * 1024 + (sub + 1) * 512],
                                    ["ysw", "wqkv"], [pkey], start=(kc == 0),
                                    stop=(kc == 7), accum=(kc > 0))
                vti = (cur * 2 + t2) * 1024
                self.sop(lambda e, vti=vti: e.copy(self.vT[:, vti:vti + 512],
                                                   t["ps_m0"][:, :]),
                         ["ps_m0"], ["vT"])
                self.sop(lambda e, vti=vti: e.copy(self.vT[:, vti + 512:vti + 1024],
                                                   t["ps_m1"][:, :]),
                         ["ps_m1"], ["vT"])
                for which in range(2):
                    psl = t["ps_a"][:, which * 1024:(which + 1) * 1024]
                    sbc = self.qsbc if which == 0 else self.ksbc
                    sbck = "qsbc" if which == 0 else "ksbc"
                    rot_d = d["rot_q"] if which == 0 else d["rot_k"]
                    self.sop(lambda e, psl=psl: e.activation(self.ttok3[:, :], psl,
                                                             AF.Square),
                             ["ps_a"], ["ttok3"])
                    self.vop(lambda e: e.tensor_reduce(
                        t["sumsq"][:, :],
                        self.ttok3[:, :].rearrange("p (h e) -> p h e", h=NH),
                        AX, ALU.add),
                        ["ttok3"], ["sumsq"])
                    self.vop(lambda e: e.tensor_scalar(
                        t["sumsq"][:, :], t["sumsq"][:, :], 1e-24, None, ALU.max),
                        ["sumsq"], ["sumsq"])
                    self.sop(lambda e: e.activation(t["sumsq"][:, :], t["sumsq"][:, :],
                                                    AF.Ln),
                             ["sumsq"], ["sumsq"])
                    self.sop(lambda e: e.activation(t["invv"][:, :], t["sumsq"][:, :],
                                                    AF.Exp, scale=-0.5),
                             ["sumsq"], ["invv"])
                    self.vop(lambda e, psl=psl: e.tensor_tensor(
                        self.ttok1[:, :], psl,
                        apbc(t["invv"][:, 0:1], [[1, NH], [0, DH]]), ALU.mult),
                        ["ps_a", "invv"], ["ttok1"])
                    self.dma(self.rotw[:, 0:128],
                             rot_d.ap()[w * WIN + t2 * 128: w * WIN + (t2 + 1) * 128, :],
                             [], ["rotw"])
                    self.vop(lambda e, sbc=sbc: e.tensor_tensor(
                        self.rots[:, 0:128], self.rotw[:, 0:128], sbc, ALU.mult),
                             ["rotw", sbck], ["rots"])
                    self.vop(lambda e: e.tensor_tensor(
                        self.ttok2[:, :], self.ttok1[:, :],
                        apbc(self.rots[:, 0:64], [[0, NH], [1, DH]]), ALU.mult),
                        ["ttok1", "rots"], ["ttok2"])
                    self.vop(lambda e: e.tensor_tensor(
                        apbc(self.ttok3[:, 0:32], [[64, NH], [1, 32]]),
                        apbc(self.ttok1[:, 32:64], [[64, NH], [1, 32]]),
                        apbc(self.rots[:, 64:96], [[0, NH], [1, 32]]), ALU.mult),
                        ["ttok1", "rots"], ["ttok3"])
                    self.vop(lambda e: e.tensor_tensor(
                        apbc(self.ttok3[:, 32:64], [[64, NH], [1, 32]]),
                        apbc(self.ttok1[:, 0:32], [[64, NH], [1, 32]]),
                        apbc(self.rots[:, 96:128], [[0, NH], [1, 32]]), ALU.mult),
                        ["ttok1", "rots"], ["ttok3"])
                    qrbuf, qrkey = (self.qrtok, "qrtok")
                    self.vop(lambda e, qrbuf=qrbuf: e.tensor_tensor(
                        qrbuf[:, :], self.ttok2[:, :], self.ttok3[:, :], ALU.add),
                        ["ttok2", "ttok3"], [qrkey])
                    for ftile in range(8):
                        self.P.op("tensor", lambda e, ftile=ftile, qrbuf=qrbuf: e.transpose(
                            t["ps_tb"][:, ftile * 128:(ftile + 1) * 128],
                            qrbuf[:, ftile * 128:(ftile + 1) * 128],
                            t["ident_bf"][:, :]),
                            reads=[qrkey, "ident_b"], writes=["ps_tb"],
                            accum=(ftile > 0), tag="p3T")
                    if which == 0:
                        dstap = apbc(self.qF[:, t2 * 128:t2 * 128 + 1], [[256, 8], [1, 128]])
                        dkey = "qF"
                    else:
                        dstap = apbc(self.kF[:, cur * 2048 + t2 * 128:cur * 2048 + t2 * 128 + 1], [[256, 8], [1, 128]])
                        dkey = "kF"
                    self.sop(lambda e, dstap=dstap: e.copy(dstap, t["ps_tb"][:, :]),
                             ["ps_tb"], [dkey])
            iters = [(c, h) for c in range(2) for h in range(NH)]

            def psS_of(i):
                if i % 2 == 0:
                    return t["ps_m0"], "ps_m0"
                return t["ps_row"], "ps_row"

            def expn_of(i):
                if i % 2 == 0:
                    return self.expn, "expn"
                return self.expnB, "expnB"

            def expTn_of(i):
                if i % 2 == 0:
                    return self.expTn, "expTn"
                return self.expTnB, "expTnB"

            def stA(i):
                c, h = iters[i]
                base, ftile = (h % 2) * 64, h // 2
                nA = 256 - c * 128
                psS, psSk = psS_of(i)
                self.mm(psS[:, 0:nA],
                        self.qF[base:base + 64, ftile * 256 + c * 128:
                                ftile * 256 + (c + 1) * 128],
                        self.kF[base:base + 64,
                                prev * 2048 + ftile * 256 + c * 128:
                                prev * 2048 + (ftile + 1) * 256],
                        ["qF", "kF"], [psSk], start=True, stop=False)
                self.zero_close(psS[:, 0:nA], psSk, min(nA, MF))
                self.mm(psS[:, nA:384],
                        self.qF[base:base + 64, ftile * 256 + c * 128:
                                ftile * 256 + (c + 1) * 128],
                        self.kF[base:base + 64,
                                cur * 2048 + ftile * 256:
                                cur * 2048 + ftile * 256 + 128 + c * 128],
                        ["qF", "kF"], [psSk], start=True, stop=False, accum=True)
                self.zero_close(psS[:, nA:384], psSk, min(384 - nA, MF))

            def stB(i):
                p = i % 2
                psS, psSk = psS_of(i)
                expnP, expnPk = expn_of(i)
                self.vop(lambda e, psS=psS, p=p: e.tensor_reduce(
                    t["mx2"][:, p:p + 1], psS[:, 0:384], AX, ALU.max,
                    negate=True),
                    [psSk], [f"mx{p}"])
                self.sop(lambda e, psS=psS, expnP=expnP, p=p: e.activation(
                    expnP[:, :], psS[:, 0:384], AF.Exp,
                    bias=t["mx2"][:, p:p + 1]),
                    [psSk, f"mx{p}"], [expnPk])

            def stC(i):
                c, h = iters[i]
                p = i % 2
                expnP, expnPk = expn_of(i)
                pat = 0 if w > 0 else (1 if c == 0 else 2)
                self.vop(lambda e, pat=pat, expnP=expnP, p=p: e.scalar_tensor_tensor(
                    expnP[:, :], expnP[:, :], 1.0,
                    t["masks_bf"][:, pat * 384:(pat + 1) * 384],
                    ALU.bypass, ALU.mult,
                    accum_out=t["dn2"][:, p:p + 1]),
                    [expnPk, "masks"], [expnPk, f"dn{p}"])
                self.vop(lambda e, p=p: e.reciprocal(t["rc2"][:, p:p + 1],
                                                     t["dn2"][:, p:p + 1]),
                         [f"dn{p}"], [f"rc{p}"])
                self.vop(lambda e, expnP=expnP, p=p: e.tensor_scalar(
                    expnP[:, :], expnP[:, :], t["rc2"][:, p:p + 1], None,
                    ALU.mult),
                    [expnPk, f"rc{p}"], [expnPk])

            def stD(i):
                c, h = iters[i]
                expnP, expnPk = expn_of(i)
                expTnP, expTnPk = expTn_of(i)
                for jc in range(3):
                    self.P.op("tensor", lambda e, jc=jc, expnP=expnP: e.transpose(
                        t["ps_tb"][:, jc * 128:(jc + 1) * 128],
                        expnP[:, jc * 128:(jc + 1) * 128],
                        t["ident_bf"][:, :]),
                        reads=[expnPk, "ident_b"], writes=["ps_tb"],
                        accum=(jc > 0), tag="eT")
                self.sop(lambda e, expTnP=expTnP: e.copy(expTnP[:, :], t["ps_tb"][:, 0:384]),
                         ["ps_tb"], [expTnPk])
                if c == 0:
                    tidx = [prev * 2 + 0, prev * 2 + 1, cur * 2 + 0]
                else:
                    tidx = [prev * 2 + 1, cur * 2 + 0, cur * 2 + 1]
                sub = h % 2
                for jc in range(3):
                    self.mm(t["ps_m1"][sub * 64:sub * 64 + 64, 0:128],
                            self.vT[:, tidx[jc] * 1024 + h * DH:
                                    tidx[jc] * 1024 + (h + 1) * DH],
                            expTnP[:, jc * 128:(jc + 1) * 128],
                            ["vT", expTnPk], ["ps_m1"], start=(jc == 0),
                            stop=(jc == 2), accum=(jc > 0), tp=(0, sub * 64))
                if sub == 1:
                    self.sop(lambda e, h=h, c=c: e.copy(
                        self.attnw[:, (h // 2) * 256 + c * 128:
                                   (h // 2) * 256 + (c + 1) * 128],
                        t["ps_m1"][:, 0:128]),
                        ["ps_m1"], ["attnw"])

            stA(0)
            stA(1)
            stB(0)
            for i in range(len(iters)):
                if i + 2 < len(iters):
                    stA(i + 2)
                if i + 1 < len(iters):
                    stB(i + 1)
                stC(i)
                stD(i)
            for mc in range(8):
                pm, pk = (t["ps_m0"], "ps_m0") if mc % 2 == 0 else (t["ps_m1"], "ps_m1")
                for kc in range(8):
                    self.mm(pm[:, 0:256],
                            t["WA"][:, kc * 1024 + mc * 128: kc * 1024 + mc * 128 + 128],
                            self.attnw[:, kc * 256:(kc + 1) * 256],
                            ["WA", "attnw"], [pk], start=(kc == 0), stop=(kc == 7),
                            accum=(kc > 0))
                self.vop(lambda e, mc=mc, pm=pm, w=w: e.tensor_tensor(
                    t["ySB"][:, mc * NTOK + w * WIN: mc * NTOK + (w + 1) * WIN],
                    t["ySB"][:, mc * NTOK + w * WIN: mc * NTOK + (w + 1) * WIN],
                    pm[:, 0:256], ALU.add),
                    ["ySB", pk], ["ySB"])

    def p3_ffn(self, li):
        t, d = self.t, self.dram
        self.claim(["ysw", "qrtok", "qF", "xn2g", "xn2gB", "hgrp", "wqkv",
                    "w1jc0", "w1jc1", "kF", "vT", "w2mc0", "w2mc1", "WA", "WB",
                    "ps_a", "ps_aL0", "ps_aL1", "ySB", "ySBg0", "ySBg1",
                    "ySBw0", "ySBw1"])
        ys2 = self.t["BIGC"][:, :].bitcast(BF16)[:, 0:4096]
        psL0 = t["ps_a"][:, 1024:1536]
        psL1 = t["ps_a"][:, 1536:2048]
        self.dma(t["lng_t"][:, :], d["lng_l"].ap()[li], [], ["lng_t"])
        self.dma(t["lnb_t"][:, :], d["lnb_l"].ap()[li], [], ["lnb_t"])
        self.dma(t["b1a_t"][:, :], d["b1a_l"].ap()[li], [], ["b1a_t"])
        self.dma(t["b1g_t"][:, :], d["b1g_l"].ap()[li], [], ["b1g_t"])
        self.dma(t["b2_t"][:, :], d["b2_l"].ap()[li], [], ["b2_t"])
        for f in range(4, NF):
            self.sop(lambda e, f=f: e.copy(
                t["edgec"][:, (f - 4) * 3:(f - 4) * 3 + 3],
                apbc(t["ySB"][:, f * NTOK + 511: f * NTOK + 512], [[512, 3]])),
                ["ySBg0", "ySBg1"], ["edgec"])

        def xg_of(g):
            if g % 2 == 0:
                return self.xn2g, "xn2g"
            return self.xn2gB, "xn2gB"

        def stP(g):
            g0 = g * 512
            yk = f"ySBg{g % 2}"
            xg, xgk = xg_of(g)
            for f in range(NF):
                dst = ys2[:, f * 512:(f + 1) * 512]
                wk = ["ysw"]
                if f < 4:
                    self.sop(lambda e, f=f, dst=dst, g0=g0: e.copy(
                        dst, t["ySB"][:, f * NTOK + g0: f * NTOK + g0 + 512]),
                        [yk], wk)
                elif g > 0:
                    self.sop(lambda e, f=f, dst=dst, g=g: e.copy(
                        dst[:, 0:1], t["edgec"][:, (f - 4) * 3 + g - 1:(f - 4) * 3 + g]),
                        ["edgec"], wk)
                    self.sop(lambda e, f=f, dst=dst, g0=g0: e.copy(
                        dst[:, 1:512], t["ySB"][:, f * NTOK + g0: f * NTOK + g0 + 511]),
                        [yk], wk)
                else:
                    self.vop(lambda e, dst=dst: e.memset(dst[:, 0:1], 0.0), [], wk)
                    self.sop(lambda e, f=f, dst=dst: e.copy(
                        dst[:, 1:512], t["ySB"][:, f * NTOK: f * NTOK + 511]),
                        [yk], ["ysw"])
            self.stats_chunk(ys2, "ysw", 0, tok_stride=512)
            self.mm(psL0[:, :], t["onesrow_f"][:, :], t["murow"][0:1, :],
                    ["murow", "ones_r"], ["ps_aL0"])
            self.mm(psL1[:, :], t["onesrow_f"][:, :], t["rstdrow"][0:1, :],
                    ["rstdrow", "ones_r"], ["ps_aL1"])
            for f in range(NF):
                self.vop(lambda e, f=f: e.tensor_tensor(
                    self.scr32[:, 0:512], ys2[:, f * 512:(f + 1) * 512],
                    psL0[:, :], ALU.subtract),
                    ["ysw", "ps_aL0"], ["scrA"])
                self.vop(lambda e: e.tensor_tensor(
                    self.scr32[:, 512:1024], self.scr32[:, 0:512], psL1[:, :],
                    ALU.mult),
                    ["scrA", "ps_aL1"], ["scrA"])
                self.vop(lambda e, f=f, xg=xg: e.tensor_scalar(
                    xg[:, f * 512:(f + 1) * 512], self.scr32[:, 512:1024],
                    t["lng_t"][:, f:f + 1], t["lnb_t"][:, f:f + 1],
                    ALU.mult, ALU.add),
                    ["scrA", "lng_t", "lnb_t"], [xgk])

        def stW(g):
            g0 = g * 512
            yk = f"ySBg{g % 2}"
            xg, xgk = xg_of(g)
            for jc in range(NJC):
                wdt = JCW[jc]
                buf = t["WB"][:, (jc % 2) * 4096:(jc % 2) * 4096 + 8 * wdt]
                bkey = f"w1jc{jc % 2}"
                self.cast_dma(buf.rearrange("p (k m) -> p k m", k=8),
                              d["w1_l"].ap()[li][:, jc * 128: jc * 128 + wdt]
                              .rearrange("(k p) m -> p k m", p=128),
                              [], [bkey], tag="w1")
                buf2 = t["WB"][:, (jc % 2) * 4096 + 2048:(jc % 2) * 4096 + 2048 + 8 * wdt]
                self.cast_dma(buf2.rearrange("p (k m) -> p k m", k=8),
                              d["w1_l"].ap()[li][:, DI + jc * 128: DI + jc * 128 + wdt]
                              .rearrange("(k p) m -> p k m", p=128),
                              [], [bkey], tag="w1g")
                for kc in range(8):
                    self.mm(t["ps_m0"][0:wdt, :], buf[:, kc * wdt:(kc + 1) * wdt],
                            xg[:, kc * 512:(kc + 1) * 512],
                            [bkey, xgk], ["ps_m0"], start=(kc == 0), stop=(kc == 7),
                            accum=(kc > 0))
                    self.mm(t["ps_m1"][0:wdt, :], buf2[:, kc * wdt:(kc + 1) * wdt],
                            xg[:, kc * 512:(kc + 1) * 512],
                            [bkey, xgk], ["ps_m1"], start=(kc == 0), stop=(kc == 7),
                            accum=(kc > 0))
                self.sop(lambda e, jc=jc, wdt=wdt: e.activation(
                    self.gelu16[0:wdt, :], t["ps_m1"][0:wdt, :], AF.Gelu,
                    bias=t["b1g_t"][0:wdt, jc:jc + 1]),
                    ["ps_m1", "b1g_t"], ["gelu16"])
                self.vop(lambda e, jc=jc, wdt=wdt: e.scalar_tensor_tensor(
                    self.hgrp[0:wdt, jc * 512:(jc + 1) * 512], t["ps_m0"][0:wdt, :],
                    t["b1a_t"][0:wdt, jc:jc + 1], self.gelu16[0:wdt, :],
                    ALU.add, ALU.mult),
                    ["ps_m0", "b1a_t", "gelu16"], ["hgrp"])
            for mc in range(8):
                wbuf = t["WA"][:, (mc % 2) * 4096:(mc % 2) * 4096 + 2816]
                wkey = f"w2mc{mc % 2}"
                self.cast_dma(wbuf[:, 0:21 * 128].rearrange("p (jc m) -> p jc m", jc=21),
                              d["w2_l"].ap()[li][0:2688, mc * 128:(mc + 1) * 128]
                              .rearrange("(jc p) m -> p jc m", p=128),
                              [], [wkey], tag="w2")
                self.cast_dma(wbuf[0:42, 21 * 128:22 * 128],
                              d["w2_l"].ap()[li][2688:2730, mc * 128:(mc + 1) * 128],
                              [wkey], [wkey], tag="w2r")
                ys = t["ps_a"][:, (mc % 2) * 512:(mc % 2 + 1) * 512]
                for jc in range(NJC):
                    wdt = JCW[jc]
                    self.mm(ys, wbuf[0:wdt, jc * 128:jc * 128 + 128],
                            self.hgrp[0:wdt, jc * 512:(jc + 1) * 512],
                            [wkey, "hgrp"], ["ps_a"], start=(jc == 0),
                            stop=(jc == NJC - 1), accum=(jc > 0))
                self.vop(lambda e, mc=mc, g0=g0, ys=ys: e.scalar_tensor_tensor(
                    t["ySB"][:, mc * NTOK + g0: mc * NTOK + g0 + 512], ys,
                    t["b2_t"][:, mc:mc + 1],
                    t["ySB"][:, mc * NTOK + g0: mc * NTOK + g0 + 512],
                    ALU.add, ALU.add),
                    ["ps_a", "b2_t", yk], [yk])

        stP(0)
        for g in range(4):
            if g + 1 < 4:
                stP(g + 1)
            stW(g)

    def p4(self):
        t, d = self.t, self.dram
        self.claim(["rotw", "rots", "xt", "xt2", "ySB", "ySBg0", "ySBg1",
                    "ySBw0", "ySBw1"])
        for f in range(NF):
            for c in range(NCH):
                fs = slice(f * NTOK + c * 512, f * NTOK + (c + 1) * 512)
                self.dma(self.xt[:, :], d["xacc_sp"].ap()[:, fs], ["xacc_sp"], ["xt"],
                         eng="sync")
                self.vop(lambda e, fs=fs: e.tensor_tensor(
                    self.xt2[:, :], self.xt[:, :], t["ySB"][:, fs], ALU.add),
                    ["xt", "ySB"], ["xt2"])
                self.dma(d["out_fm"].ap()[:, fs], self.xt2[:, :], ["xt2"], ["out"],
                         eng="sync")
        self.P.wait_all("sync", self.P.barrier_token())

    def finish(self):
        self.P.emit(self.nc, self.sems)
        self.es.close()
        return self.nc


def rotary_tables_np():
    F32n = np.float32
    half = np.arange(0, DH, 2, dtype=F32n)
    inv_freq = (1.0 / 10000.0 ** (half / DH)).astype(F32n)
    tt = np.arange(NTOK, dtype=F32n)
    freqs = tt[:, None] * inv_freq[None, :]
    freqs = np.concatenate([freqs, freqs], -1)
    base = ((half + 0.4 * DH) / (1.4 * DH)).astype(F32n)
    power = ((tt - NTOK // 2) / (WIN // 2)).astype(F32n)
    scale = base[None, :] ** power[:, None]
    scale = np.concatenate([scale, scale], -1)
    cos, sin = np.cos(freqs).astype(F32n), np.sin(freqs).astype(F32n)
    sgn = np.concatenate([-np.ones(DH // 2, F32n), np.ones(DH // 2, F32n)])
    rot_q = np.concatenate([cos * scale * 8.0, sin * scale * sgn * 8.0], 1)
    rot_k = np.concatenate([cos / scale, sin / scale * sgn], 1)
    return np.ascontiguousarray(rot_q), np.ascontiguousarray(rot_k)


def band_masks_np():
    F32n = np.float32
    ii = np.arange(128)[:, None]
    jj = np.arange(384)[None, :]
    band = ((jj >= ii) & (jj <= ii + 256)).astype(F32n)
    w0c0 = band * (jj >= 256)
    w0c1 = band * (jj >= 128)
    return np.ascontiguousarray(np.concatenate([band, w0c0, w0c1], 1))


def prep_shared_inputs(I):
    F32n = np.float32
    out = {}
    x = np.asarray(I["L0_attn_x"], F32n)
    out["xall"] = np.ascontiguousarray(x.transpose(0, 2, 1))
    for w in ("wq", "wk", "wv", "wo"):
        out[w] = np.ascontiguousarray(np.asarray(I["L0_attn_" + w], F32n))
    out["projT"] = np.ascontiguousarray(np.asarray(I["L0_attn_proj"], F32n).T)
    out["bo_row"] = np.ascontiguousarray(np.asarray(I["L0_attn_bo"], F32n)[None, :])
    cst = np.zeros((128, 16), F32n); cst[:, 0] = 1.0
    out["cst"] = cst
    cr = np.zeros((4, 128), F32n); cr[0] = 1.0
    out["cstrow"] = cr
    out["ones512"] = np.ones((1, 512), F32n)
    out["ident"] = np.eye(128, dtype=F32n)
    ep = np.zeros((128, 8, 16), F32n)
    for t_ in range(8):
        for p in range(128):
            ep[p, t_, 2 * t_ + p // 64] = 1.0
    out["epat"] = np.ascontiguousarray(ep.reshape(128, 128))
    out["masks"] = band_masks_np()
    out["wqkv_l"] = np.ascontiguousarray(np.asarray(I["L0_ffn_Wqkv"], F32n))
    out["wo_l"] = np.ascontiguousarray(np.asarray(I["L0_ffn_Wo"], F32n))
    out["w1_l"] = np.ascontiguousarray(np.asarray(I["L0_ffn_W1"], F32n))
    out["w2_l"] = np.ascontiguousarray(np.asarray(I["L0_ffn_W2"], F32n))
    qs = np.asarray(I["L0_ffn_q_scale"], F32n)
    ks = np.asarray(I["L0_ffn_k_scale"], F32n)
    out["qsrows"] = np.ascontiguousarray(np.concatenate(
        [qs, np.concatenate([qs[:, DH // 2:], qs[:, :DH // 2]], 1)], 1)[:, None, :])
    out["ksrows"] = np.ascontiguousarray(np.concatenate(
        [ks, np.concatenate([ks[:, DH // 2:], ks[:, :DH // 2]], 1)], 1)[:, None, :])
    lng = np.asarray(I["L0_ffn_ln_g"], F32n).reshape(DEPTH, 8, 128)
    lnb = np.asarray(I["L0_ffn_ln_b"], F32n).reshape(DEPTH, 8, 128)
    out["lng_l"] = np.ascontiguousarray(lng.transpose(0, 2, 1))
    out["lnb_l"] = np.ascontiguousarray(lnb.transpose(0, 2, 1))
    b1 = np.asarray(I["L0_ffn_b1"], F32n)
    b1a = np.zeros((DEPTH, 22 * 128), F32n); b1a[:, :DI] = b1[:, :DI]
    b1g = np.zeros((DEPTH, 22 * 128), F32n); b1g[:, :DI] = b1[:, DI:]
    out["b1a_l"] = np.ascontiguousarray(b1a.reshape(DEPTH, 22, 128).transpose(0, 2, 1))
    out["b1g_l"] = np.ascontiguousarray(b1g.reshape(DEPTH, 22, 128).transpose(0, 2, 1))
    b2 = np.asarray(I["L0_ffn_b2"], F32n).reshape(DEPTH, 8, 128)
    out["b2_l"] = np.ascontiguousarray(b2.transpose(0, 2, 1))
    rq, rk = rotary_tables_np()
    out["rot_q"], out["rot_k"] = rq, rk
    return out


def make_in_maps(I):
    shared = prep_shared_inputs(I)
    x = np.asarray(I["L0_attn_x"], np.float32)
    maps = []
    for b in range(8):
        m = dict(shared)
        m["xfm"] = np.ascontiguousarray(x[b].T)
        maps.append(m)
    return maps


from concourse.bass_utils import run_bass_kernel_spmd


def kernel(**inputs):
    kb = KB()
    kb.declare()
    kb.p0()
    kb.p1()
    kb.p2_sweep()
    kb.p2_q()
    kb.p2_kv()
    kb.p2_out()
    for li in range(DEPTH):
        kb.p3_attn(li)
        kb.p3_ffn(li)
    kb.p4()
    nc = kb.finish()
    in_maps = make_in_maps(inputs)
    res = run_bass_kernel_spmd(nc, in_maps, core_ids=list(range(8)))
    out = np.empty((8, 2048, 1024), np.float32)
    for b in range(8):
        o = np.asarray(res.results[b]["out_fm"])
        out[b] = o.reshape(128, NF, NTOK).transpose(1, 0, 2).reshape(1024, NTOK).T
    return out
